# version 12
# baseline (speedup 1.0000x reference)
import numpy as np
import ml_dtypes
from contextlib import ExitStack

import concourse.bass as bass
import concourse.mybir as mybir
from concourse.bass_utils import run_bass_kernel_spmd

F32 = mybir.dt.float32
BF16 = mybir.dt.bfloat16
AF = mybir.ActivationFunctionType
ALU = mybir.AluOpType
AX = mybir.AxisListType
BFD = ml_dtypes.bfloat16

H, DKH, DVH, D = 8, 256, 512, 2048
EPS = 1e-5
Bsz, T = 2, 4096
TPC, NT = 1024, 8
PRE, PREH, NPH = 3072, 1536, 12
NPT = PRE // 128
FCH = 16
C = 128
GAMMAS = [float(1.0 - 2.0 ** (-5.0 - h)) for h in range(H)]
WBLK = 512
DEBUG = False
DEBUG_SET = None
PHASE_MARKS = []


class Plan:
    ENGS = ("sync", "scalar", "vector", "tensor", "gpsimd")

    def __init__(self, nc, es):
        self.nc = nc
        self.streams = {e: [] for e in self.ENGS}
        self.sems = {}
        self.last_wait = {}
        self.es = es

    def sem(self, name):
        h = self.es.enter_context(self.nc.semaphore(name))
        self.sems[name] = [h, 0]
        return name

    def cur(self, name):
        return self.sems[name][1]

    def emit(self, eng, fn):
        self.streams[eng].append(fn)

    def wait(self, eng, sem, v):
        if v is None or v <= 0:
            return
        assert v <= self.cur(sem), (eng, sem, v, self.cur(sem))
        if self.last_wait.get((eng, sem), -1) >= v:
            return
        self.last_wait[(eng, sem)] = v
        h = self.sems[sem][0]
        self.emit(eng, lambda e, h=h, v=v: e.wait_ge(h, v))

    def op(self, eng, build, sem=None, k=1, wide=False):
        v, h = None, None
        if sem is not None:
            s = self.sems[sem]
            s[1] += k
            v, h = s[1], s[0]
        if sem is not None and eng == "scalar":
            self.emit(eng, lambda e, build=build: build(e))
            self.emit(eng, lambda e, h=h, k=k: e.copy(
                self.dummy_s[0], self.dummy_s[1]).then_inc(h, k))
        elif sem is not None and eng == "vector":
            self.emit(eng, lambda e, build=build: build(e))
            if wide:
                p = self.pad_vs if wide == "s" else self.pad_v
                self.emit(eng, lambda e, p=p: e.tensor_copy(p[0], p[1]))
            self.emit(eng, lambda e, h=h, k=k: e.tensor_copy(
                self.dummy_v[0], self.dummy_v[1]).then_inc(h, k))
        else:
            def fn(e, build=build, h=h, k=k):
                ins = build(e)
                if h is not None:
                    ins.then_inc(h, k)
            self.emit(eng, fn)
        return v

    def dma(self, eng, out, in_, sem, **kw):
        return self.op(eng, lambda e, o=out, i=in_, kw=kw: e.dma_start(out=o, in_=i, **kw),
                       sem, 16)


def bc_free(ap, n):
    return bass.AP(ap.tensor, ap.offset, [list(ap.ap[0]), [0, n]])


def build_nc():
    nc = bass.Bass("TRN2", target_bir_lowering=False, debug=False)

    def din(name, shape, dt=F32):
        return nc.dram_tensor(name, shape, dt, kind="ExternalInput").ap()

    x_own = din("x_own", [D, TPC])
    x_pre = din("x_pre", [D, PRE])
    x_own_tm = din("x_own_tm", [TPC, D])
    x_pre_tm = din("x_pre_tm", [PRE, D])
    wq = din("wq", [D, D])
    wk = din("wk", [D, D])
    wv = din("wv", [D, 2 * D])
    wg = din("wg", [D, 2 * D])
    wo = din("wo", [2 * D, D])
    wrt = din("wrt", [D, H])
    br_row = din("br_row", [1, H])
    gw_row = din("gw_row", [1, DVH])
    at_p = din("at_p", [D, 512])
    bcat = din("bcat", [512, D])
    cos_own = din("cos_own", [128, NT * 128], BF16)
    sin_own = din("sin_own", [128, NT * 128], BF16)
    cosq = din("cosq", [128, TPC], BF16)
    sinq = din("sinq", [128, TPC], BF16)
    cos_pre = din("cos_pre", [128, NPT * 128], BF16)
    sin_pre = din("sin_pre", [128, NPT * 128], BF16)
    dt_mask = din("dt_mask", [128, H * 128], BF16)
    gam_tab = din("gam_tab", [128, H * 128], BF16)
    kgam_v = din("kgam_v", [128, H])
    ugam_v = din("ugam_v", [128, H * NPT])
    ones_bf = din("ones_bf", [128, 128], BF16)
    constv = din("constv", [128, 16])
    ones_f = din("ones_f", [1, 128])
    ident_f = din("ident_f", [128, 128])
    expand_bf = din("expand_bf", [8, 512], BF16)

    sp_k_pre = nc.dram_tensor("sp_k_pre", [PRE, D], BF16, kind="Internal")
    sp_v_pre = nc.dram_tensor("sp_v_pre", [PRE, 2 * D], BF16, kind="Internal")
    sp_k_own = nc.dram_tensor("sp_k_own", [TPC, D], BF16, kind="Internal")
    sp_v_own = nc.dram_tensor("sp_v_own", [TPC, 2 * D], BF16, kind="Internal")
    sp_g_own = nc.dram_tensor("sp_g_own", [TPC, 2 * D], BF16, kind="Internal")
    sp_og = nc.dram_tensor("sp_og", [TPC, 2 * D], BF16, kind="Internal")
    sp_stat = nc.dram_tensor("sp_stat", [2, PRE], F32, kind="Internal")
    bc_scr = nc.dram_tensor("bc_scr", [40, 512], F32, kind="Internal")
    out_fm = nc.dram_tensor("out_fm", [D, TPC], F32, kind="ExternalOutput").ap()
    dbg = {}
    if DEBUG:
        sel = DEBUG_SET
        for nm, shape, dt in [("d_xn", [128, 16384], BF16), ("d_q", [128, 16384], BF16),
                              ("d_u", [128, 8192], F32), ("d_kown", [TPC, D], BF16),
                              ("d_vown", [TPC, 2 * D], BF16), ("d_gown", [TPC, 2 * D], BF16),
                              ("d_kpre", [PRE, D], BF16), ("d_vpre", [PRE, 2 * D], BF16),
                              ("d_og", [TPC, 2 * D], BF16), ("d_x1", [128, 16384], F32),
                              ("d_oraw", [128, 4096], F32),
                              ("d_xnpreA", [128, 24576], BF16), ("d_bc", [128, 3072], BF16),
                              ("d_xpreraw", [128, 24576], BF16), ("d_scr", [40, 512], F32),
                              ("d_xn2", [128, 16384], BF16), ("d_lnbc", [128, 4096], BF16), ("d_obc", [128, 4096], BF16),
                              ("d_ps", [128, 5120], F32),
                              ("d_ps2", [128, 2048], F32), ("d_ps3", [128, 4096], BF16),
                              ("d_r", [128, 32], F32), ("d_f", [128, 12288], BF16),
                              ("d_m", [128, 64], F32), ("d_stat", [128, 16], F32)]:
            if sel is not None and nm not in sel:
                continue
            dbg[nm] = nc.dram_tensor(nm, shape, dt, kind="ExternalOutput").ap()

    es = ExitStack()
    S = es.enter_context

    PA = S(nc.sbuf_tensor("PA", [128, 32768], BF16))
    PB = S(nc.sbuf_tensor("PB", [128, 16384], BF16))
    PQ = S(nc.sbuf_tensor("PQ", [128, 16384], BF16))
    PW = S(nc.sbuf_tensor("PW", [128, 16384], BF16))
    PE2 = S(nc.sbuf_tensor("PE2", [128, 4096], BF16))
    PS = S(nc.sbuf_tensor("PS", [128, 4096], BF16))
    PT = S(nc.sbuf_tensor("PT", [128, 9856], BF16))
    PV = S(nc.sbuf_tensor("PV", [128, 4096], BF16))
    ps = [S(nc.psum_tensor(f"psb{b}", [128, 512], F32)) for b in range(8)]

    PAf = PA.bitcast(F32)
    PBf = PB.bitcast(F32)
    PQf = PQ.bitcast(F32)
    PE2f = PE2.bitcast(F32)
    PSf = PS.bitcast(F32)
    PVf = PV.bitcast(F32)

    P = Plan(nc, es)
    P.dummy_s = (PSf[:, 2040:2041], PSf[:, 2042:2043])
    P.dummy_v = (PSf[:, 2044:2045], PSf[:, 2046:2047])
    P.pad_v = (PSf[:, 1536:2040], PSf[:, 1024:1528])
    P.pad_vs = (PSf[:, 1536:1664], PSf[:, 1024:1152])
    LD = P.sem("LD")
    SY = P.sem("SY")
    MM = P.sem("MM")
    EV = P.sem("EV")
    DV = P.sem("DV")

    PTo, off = {}, 0
    for nm, w in [("cos_own", NT * 128), ("sin_own", NT * 128),
                  ("cos_pre", NPH * 128), ("sin_pre", NPH * 128),
                  ("dt", H * 128), ("gam", H * 128), ("ones", 128),
                  ("cosq", TPC), ("sinq", TPC), ("expand", 512)]:
        PTo[nm] = off
        off += w
    assert off <= 9856, off

    def Tb(nm, w, o=0):
        return PT[:, PTo[nm] + o:PTo[nm] + o + w]

    ident_sb = PSf[:, 0:128]
    kgam_sb = PSf[:, 128:136]
    ugam_sb = PSf[:, 144:144 + H * NPT]
    br_bc = PSf[:, 336:344]
    ones_f_sb = PVf[0:1, 0:128]
    m_slot = PVf[0:1, 128:640]
    r_slot = PVf[0:1, 640:1152]
    br_raw = PVf[0:1, 1152:1160]
    rms_acc = PVf[:, 1680:1688]
    rms_rstd = PVf[:, 1688:1696]
    gw_raw = PV[0:1, 3392:3392 + DVH]
    wrt_sb = PV[:, 3904:4032]

    CVALS = [EPS, 1.0 / D, -0.5, 1.0 / DVH, 1.0 / 64.0, 0.0, 1.0, 2.0] + \
        [GAMMAS[h] ** C for h in range(H)]
    constv_sb = PSf[:, 352:352 + len(CVALS)]

    bank_ev = {b: 0 for b in range(8)}
    bank_dv = {b: 0 for b in range(8)}

    PHASE_MARKS.clear()

    def mark(name):
        PHASE_MARKS.append((name, {s: P.cur(s) for s in ("LD", "SY", "MM", "EV", "DV")}))

    def bank_wait(bank):
        P.wait("tensor", EV, bank_ev[bank])
        P.wait("tensor", DV, bank_dv[bank])

    def full_barrier():
        for e in Plan.ENGS:
            for sm in (LD, SY, MM, EV, DV):
                P.wait(e, sm, P.cur(sm))

    def dbg_dump(name, src):
        if not DEBUG or name not in dbg:
            return
        v = P.dma("sync", dbg[name], src, SY)
        for e in Plan.ENGS:
            P.wait(e, SY, v)

    def dbg_dump_dram(name, src_dram):
        if not DEBUG or name not in dbg:
            return
        v = P.dma("gpsimd", dbg[name], src_dram.ap()[:, :], LD)
        for e in Plan.ENGS:
            P.wait(e, LD, v)

    for dst, src in [(Tb("cos_own", NT * 128), cos_own), (Tb("sin_own", NT * 128), sin_own),
                     (Tb("dt", H * 128), dt_mask), (Tb("gam", H * 128), gam_tab),
                     (Tb("ones", 128), ones_bf), (Tb("cosq", TPC), cosq),
                     (Tb("sinq", TPC), sinq)]:
        P.dma("gpsimd", dst, src, LD)
    P.dma("gpsimd", PT[0:8, PTo["expand"]:PTo["expand"] + 512], expand_bf, LD)
    P.dma("gpsimd", ident_sb, ident_f, LD)
    P.dma("gpsimd", ugam_sb, ugam_v, LD)
    P.dma("gpsimd", kgam_sb, kgam_v, LD)
    P.dma("gpsimd", constv_sb, constv, LD)
    for j, val in enumerate(CVALS):
        nc.const_aps.register(F32, val, constv_sb[:, j:j + 1])
    P.dma("gpsimd", br_raw, br_row, LD)
    P.dma("gpsimd", gw_raw, gw_row, LD)
    P.dma("gpsimd", wrt_sb.rearrange("p (f c) -> p f c", c=H),
          wrt.rearrange("(f p) c -> p f c", p=128), LD)
    v_tab = P.dma("gpsimd", ones_f_sb, ones_f, LD)
    for _e in ("sync", "scalar", "vector", "tensor"):
        P.wait(_e, LD, v_tab)

    def stats_tm_accum(xtm, tok0, ntiles):
        ring = 4
        scol = PVf[:, 1160:1160 + ntiles]
        qcol = PVf[:, 1184:1184 + ntiles]
        mcol = PVf[:, 1208:1208 + ntiles]
        vcol = PVf[:, 1232:1232 + ntiles]
        tcol = PVf[:, 1256:1256 + ntiles]
        dump = PV[:, 256:2304]
        vac = [0] * ntiles
        vld = {}
        P.wait("sync", EV, P.cur(EV))

        def tm_load(t):
            if t < 0 or t >= ntiles:
                return
            if t >= ring:
                P.wait("sync", EV, vac[t - ring])
            vld[t] = P.dma("sync", PQf[:, (t % ring) * 2048:(t % ring) * 2048 + 2048],
                           xtm[tok0 + t * 128:tok0 + (t + 1) * 128, :], SY)

        tm_load(0)
        tm_load(1)
        for t in range(ntiles):
            tm_load(t + 2)
            P.wait("scalar", SY, vld[t])
            sl = (t % ring) * 2048
            P.op("scalar", lambda s, sl=sl, t=t: s.activation(
                dump, PQf[:, sl:sl + 2048], AF.Copy,
                accum_out=scol[:, t:t + 1]))
            vac[t] = P.op("scalar", lambda s, sl=sl, t=t: s.activation(
                dump, PQf[:, sl:sl + 2048], AF.Square,
                accum_out=qcol[:, t:t + 1]), EV)
        vml = P.op("scalar", lambda s: s.activation(mcol, scol, AF.Copy,
                                                    scale=1.0 / D), EV)
        P.wait("vector", EV, vml)
        P.op("vector", lambda v: v.tensor_mul(tcol, mcol, mcol), DV, wide=True)
        P.op("vector", lambda v: v.tensor_scalar_mul(vcol, qcol, 1.0 / D), DV, wide=True)
        vvar = P.op("vector", lambda v: v.tensor_sub(vcol, vcol, tcol), DV, wide=True)
        P.wait("scalar", DV, vvar)
        P.op("scalar", lambda s: s.activation(vcol, vcol, AF.Ln, bias=EPS))
        return P.op("scalar", lambda s: s.activation(vcol, vcol, AF.Exp,
                                                     scale=-0.5), EV)

    def stats_tm_bcast(ntiles, mdst, rdst, dst_f32, vr):
        mcol = PVf[:, 1208:1208 + ntiles]
        vcol = PVf[:, 1232:1232 + ntiles]
        ntok = ntiles * 128
        m_t = PVf[0:ntiles, 1280:1408]
        r_t = PVf[0:ntiles, 1408:1536]
        P.wait("tensor", EV, vr)
        P.wait("tensor", LD, v_tab)
        bank_wait(6)
        bank_wait(7)
        P.op("tensor", lambda t: t.transpose(ps[6][0:ntiles, 0:128], mcol, ident_sb))
        P.op("tensor", lambda t: t.transpose(ps[7][0:ntiles, 0:128], vcol, ident_sb))
        vtr = P.op("tensor", lambda t: t.drain(), MM)
        P.wait("scalar", MM, vtr)
        P.op("scalar", lambda s: s.copy(m_t, ps[6][0:ntiles, 0:128]))
        vev = P.op("scalar", lambda s: s.copy(r_t, ps[7][0:ntiles, 0:128]), EV)
        bank_ev[6] = vev
        bank_ev[7] = vev
        P.wait("sync", EV, vev)
        o_m = bass.AP(sp_stat, 0, [[128, ntiles], [1, 128]])
        o_r = bass.AP(sp_stat, PRE, [[128, ntiles], [1, 128]])
        P.dma("sync", o_m, m_t, SY)
        vsp = P.dma("sync", o_r, r_t, SY)
        P.wait("sync", SY, vsp)
        i_m = bass.AP(sp_stat, 0, [[0, 128], [1, ntok]])
        i_r = bass.AP(sp_stat, PRE, [[0, 128], [1, ntok]])
        if dst_f32:
            P.dma("sync", mdst, i_m, SY)
            return P.dma("sync", rdst, i_r, SY)
        stage_m = PQf[:, 0:ntok]
        stage_r = PQf[:, 2048:2048 + ntok]
        P.dma("sync", stage_m, i_m, SY)
        vbb = P.dma("sync", stage_r, i_r, SY)
        P.wait("vector", SY, vbb)
        P.op("vector", lambda v: v.tensor_copy(mdst, stage_m))
        return P.op("vector", lambda v: v.tensor_copy(rdst, stage_r), DV, wide=True)

    last_bcast_mm = [0, 0]
    scr_ctr = [0]

    def combine_and_bcast(bank2, mdst, rdst, vmm, dst_f32, dbg_ps=False):
        bank_sum, bank_sq = bank2
        m_f = PSf[:, 512:1024]
        e2_f = PSf[:, 1024:1536]
        P.wait("vector", MM, vmm)
        P.wait("vector", EV, last_bcast_mm[1])
        P.op("vector", lambda v: v.tensor_scalar_mul(m_f, ps[bank_sum][:, :], 1.0 / D))
        ve = P.op("vector", lambda v: v.tensor_scalar_mul(e2_f, ps[bank_sq][:, :], 1.0 / D), DV)
        bank_dv[bank_sum] = ve
        bank_dv[bank_sq] = ve
        if DEBUG and dbg_ps and "d_ps" in dbg:
            sl = scr_ctr[0]
            scr_ctr[0] += 1
            P.op("vector", lambda v, sl=sl: v.tensor_copy(
                PQf[:, 3072 + sl * 1024:3072 + sl * 1024 + 512], m_f))
            P.op("vector", lambda v, sl=sl: v.tensor_copy(
                PQf[:, 3072 + sl * 1024 + 512:3072 + sl * 1024 + 1024], e2_f))
        sqv = PVf[:, 1160:1672]
        P.op("vector", lambda v: v.tensor_mul(sqv, m_f, m_f))
        vv = P.op("vector", lambda v: v.tensor_sub(e2_f, e2_f, sqv), DV)
        P.wait("scalar", DV, vv)
        P.op("scalar", lambda s: s.activation(e2_f, e2_f, AF.Ln, bias=EPS))
        vr = P.op("scalar", lambda s: s.activation(e2_f, e2_f, AF.Exp, scale=-0.5), EV)
        P.wait("vector", EV, vr)
        P.op("vector", lambda v: v.tensor_copy(mdst, m_f))
        vb = P.op("vector", lambda v: v.tensor_copy(rdst, e2_f), DV)
        last_bcast_mm[0] = vb
        last_bcast_mm[1] = vr
        return vb

    mbc_own = PAf[:, 12288:13312]
    rbc_own = PAf[:, 13312:14336]
    mbc_pre = PA[:, 28672:30208]
    rbc_pre = PA[:, 30208:31744]

    mbc_own = PAf[:, 12288:13312]
    rbc_own = PAf[:, 13312:14336]
    mbc_pre = PA[:, 28672:30208]
    rbc_pre = PA[:, 30208:31744]

    def phase0(half, vr_pre=None):
        if half == 0:
            vr_o = stats_tm_accum(x_own_tm, 0, NT)
            stats_tm_bcast(NT, mbc_own, rbc_own, True, vr_o)
            vr_p = stats_tm_accum(x_pre_tm, 0, NPH)
            stats_tm_bcast(NPH, mbc_pre, rbc_pre, False, vr_p)
        else:
            stats_tm_bcast(NPH, mbc_pre, rbc_pre, False, vr_pre)
        P.wait("gpsimd", MM, P.cur(MM))
        P.wait("gpsimd", DV, P.cur(DV))
        vload = []
        for f in range(FCH):
            vload.append(P.dma("gpsimd", PA[:, f * PREH:f * PREH + PREH],
                               x_pre[f * 128:(f + 1) * 128,
                                     half * PREH:half * PREH + PREH], LD))
        vload.append(gfence())
        if DEBUG and half == 0 and "d_xpreraw" in dbg:
            for e in Plan.ENGS:
                P.wait(e, LD, vload[-1])
            dbg_dump("d_xpreraw", PA[:, 0:24576])
        vout = 0
        P.wait("vector", SY, P.cur(SY))
        P.wait("vector", LD, vload[-1])
        if half == 0:
            for f in range(FCH):
                vx = P.dma("sync", PE2f[:, 0:1024], x_own[f * 128:(f + 1) * 128, :], SY)
                P.wait("vector", SY, vx)
                P.op("vector", lambda v: v.tensor_sub(PE2f[:, 1024:2048], PE2f[:, 0:1024], mbc_own))
                vout = P.op("vector", lambda v, f=f: v.tensor_mul(
                    PB[:, f * 1024:(f + 1) * 1024], PE2f[:, 1024:2048], rbc_own), DV,
                    wide=True)
                P.wait("sync", DV, vout)
        for f in range(FCH):
            P.op("vector", lambda v, f=f: v.tensor_sub(
                PE2[:, 0:1536], PA[:, f * PREH:f * PREH + PREH], mbc_pre))
            vout = P.op("vector", lambda v, f=f: v.tensor_mul(
                PA[:, f * PREH:f * PREH + PREH], PE2[:, 0:1536], rbc_pre), DV,
                wide=True)
        return vout

    wbuf_free = [0, 0]
    stage_spill = [0, 0]
    fence_sb = PSf[0:1, 344:352]

    def gfence():
        return P.dma("gpsimd", fence_sb, x_own[0:1, 0:8], LD)

    def load_wblock(wap, col0, bi):
        P.wait("gpsimd", MM, wbuf_free[bi])
        vs = 0
        for f in range(FCH):
            vs = P.dma("gpsimd", PW[:, bi * 8192 + f * WBLK:bi * 8192 + (f + 1) * WBLK],
                       wap[f * 128:(f + 1) * 128, col0:col0 + WBLK], LD)
        return vs

    def strided2(ap, lo):
        return bass.AP(ap.tensor, ap.offset + lo, [list(ap.ap[0]), [256, 2], [1, 128]])

    def tab_bc2(o):
        t = PT[:, o:o + 128]
        return bass.AP(t.tensor, t.offset, [list(t.ap[0]), [0, 2], [1, 128]])

    def rope_tm(raw, out, cosc, sinc, sem=None):
        x1, x2 = strided2(raw, 0), strided2(raw, 128)
        o1, o2 = strided2(out, 0), strided2(out, 128)
        ct, st = tab_bc2(cosc), tab_bc2(sinc)
        t1 = strided2(PE2[:, 2048:2560], 0)
        t2 = strided2(PE2[:, 2560:3072], 0)
        P.op("vector", lambda v: v.tensor_mul(t1, x1, ct))
        P.op("vector", lambda v: v.tensor_mul(t2, x2, st))
        P.op("vector", lambda v: v.tensor_sub(o1, t1, t2))
        P.op("vector", lambda v: v.tensor_mul(t1, x2, ct))
        P.op("vector", lambda v: v.tensor_mul(t2, x1, st))
        return P.op("vector", lambda v: v.tensor_add(o2, t1, t2), DV)

    def proj(wap, ncols, tiles, kind, v_xn_ready):
        nblk = ncols // WBLK
        vwn = [None] * (nblk + 1)
        vwn[0] = load_wblock(wap, 0, 0)
        for blk in range(nblk):
            bi = blk % 2
            vwn[blk + 1] = load_wblock(wap, (blk + 1) * WBLK, (blk + 1) % 2) \
                if blk + 1 < nblk else gfence()
            P.wait("tensor", LD, vwn[blk + 1])
            P.wait("tensor", DV, v_xn_ready)
            for ti, tl in enumerate(tiles):
                bank = ti % 2
                bank_wait(bank)
                pool, stride, tt = tl["pool"], tl["stride"], tl["tt"]
                last = [None]
                for f in range(FCH):
                    P.op("tensor", lambda t, f=f, pool=pool, stride=stride, tt=tt, bi=bi,
                         bank=bank: t.matmul(
                             ps[bank][:, :],
                             pool[:, f * stride + tt * 128: f * stride + (tt + 1) * 128],
                             PW[:, bi * 8192 + f * WBLK:bi * 8192 + (f + 1) * WBLK],
                             start=(f == 0), stop=(f == FCH - 1)))
                vmm = P.op("tensor", lambda t: t.drain(), MM)
                if ti == len(tiles) - 1:
                    wbuf_free[bi] = vmm
                sl = ti % 2
                ro = sl * 1024
                P.wait("scalar", MM, vmm)
                P.wait("scalar", SY, stage_spill[sl])
                vev = P.op("scalar", lambda s, ro=ro, bank=bank: s.copy(
                    PE2[:, ro:ro + 512], ps[bank][:, :]), EV)
                bank_ev[bank] = vev
                if kind == "k":
                    P.wait("vector", EV, vev)
                    P.wait("vector", SY, stage_spill[sl])
                    vr = rope_tm(PE2[:, ro:ro + 512], PE2[:, ro + 512:ro + 1024],
                                 tl["cosc"], tl["sinc"])
                    src = PE2[:, ro + 512:ro + 1024]
                    P.wait("sync", DV, vr)
                else:
                    src = PE2[:, ro:ro + 512]
                    P.wait("sync", EV, vev)
                vsp = P.dma("sync", tl["spill"][tl["srow"]:tl["srow"] + 128,
                                                blk * WBLK:(blk + 1) * WBLK], src, SY)
                stage_spill[sl] = vsp

    def tiles_own(spill):
        return [dict(pool=PB, stride=1024, tt=t, spill=spill, srow=t * 128,
                     cosc=PTo["cos_own"] + t * 128, sinc=PTo["sin_own"] + t * 128)
                for t in range(NT)]

    def tiles_pre(spill, half):
        return [dict(pool=PA, stride=PREH, tt=t, spill=spill,
                     srow=half * PREH + t * 128,
                     cosc=PTo["cos_pre"] + t * 128, sinc=PTo["sin_pre"] + t * 128)
                for t in range(NPH)]

    mark("phase0a")
    v_xn = phase0(0)
    if DEBUG and ("d_xnpreA" in dbg or "d_bc" in dbg or "d_obc" in dbg or "d_scr" in dbg):
        for e in Plan.ENGS:
            P.wait(e, DV, v_xn)
        dbg_dump("d_xnpreA", PA[:, 0:24576])
        dbg_dump("d_bc", PA[:, 28672:31744])
        dbg_dump("d_obc", PA[:, 24576:28672])
        dbg_dump_dram("d_scr", bc_scr)
    if DEBUG and ("d_xn" in dbg or "d_stat" in dbg):
        for e in Plan.ENGS:
            P.wait(e, DV, v_xn)
        dbg_dump("d_xn", PB[:, :])
        dbg_dump("d_stat", PSf[:, 352:368].bitcast(F32) if False else constv_sb[:, 0:16])
    vct = P.dma("gpsimd", Tb("cos_pre", NPH * 128), cos_pre[:, 0:NPH * 128], LD)
    vct = P.dma("gpsimd", Tb("sin_pre", NPH * 128), sin_pre[:, 0:NPH * 128], LD)
    vct = gfence()
    P.wait("vector", LD, vct)
    mark("projA_k")
    proj(wk, D, tiles_own(sp_k_own) + tiles_pre(sp_k_pre, 0), "k", v_xn)
    mark("projA_v")
    proj(wv, 2 * D, tiles_own(sp_v_own) + tiles_pre(sp_v_pre, 0), "v", v_xn)

    mark("phase0b")
    v_xnB = phase0(1, stats_tm_accum(x_pre_tm, PREH, NPH))
    P.wait("gpsimd", DV, P.cur(DV))
    vct = P.dma("gpsimd", Tb("cos_pre", NPH * 128), cos_pre[:, NPH * 128:], LD)
    vct = P.dma("gpsimd", Tb("sin_pre", NPH * 128), sin_pre[:, NPH * 128:], LD)
    vct = gfence()
    P.wait("vector", LD, vct)
    mark("projB_k")
    proj(wk, D, tiles_pre(sp_k_pre, 1), "k", v_xnB)
    mark("projB_v")
    proj(wv, 2 * D, tiles_pre(sp_v_pre, 1), "v", v_xnB)

    if DEBUG and ("d_kpre" in dbg or "d_vpre" in dbg):
        P.wait("gpsimd", SY, P.cur(SY))
        dbg_dump_dram("d_kpre", sp_k_pre)
        dbg_dump_dram("d_vpre", sp_v_pre)

    mark("proj_g")
    proj(wg, 2 * D, tiles_own(sp_g_own), "g", v_xn)

    def q_proj():
        P.wait("scalar", SY, P.cur(SY))
        P.wait("vector", SY, P.cur(SY))
        vq = 0
        vwn = [None] * 5
        vwn[0] = load_wblock(wq, 0, 0)
        for blk in range(4):
            bi = blk % 2
            vwn[blk + 1] = load_wblock(wq, (blk + 1) * WBLK, (blk + 1) % 2) \
                if blk + 1 < 4 else gfence()
            P.wait("tensor", LD, vwn[blk + 1])
            for hh in range(2):
                h = blk * 2 + hh
                for s in range(2):
                    for th in range(2):
                        bank = 2 * s + th
                        bank_wait(bank)
                        for f in range(FCH):
                            P.op("tensor", lambda t, f=f, hh=hh, s=s, th=th, bi=bi, bank=bank:
                                 t.matmul(ps[bank][:, :],
                                          PW[:, bi * 8192 + f * WBLK + hh * 256 + s * 128:
                                             bi * 8192 + f * WBLK + hh * 256 + s * 128 + 128],
                                          PB[:, f * 1024 + th * 512: f * 1024 + (th + 1) * 512],
                                          start=(f == 0), stop=(f == FCH - 1)))
                vmm = P.op("tensor", lambda t: t.drain(), MM)
                if hh == 1:
                    wbuf_free[bi] = vmm
                P.wait("scalar", MM, vmm)
                P.wait("scalar", DV, vq)
                vev = 0
                for s in range(2):
                    for th in range(2):
                        bank = 2 * s + th
                        vev = P.op("scalar", lambda sc, s=s, th=th, bank=bank: sc.copy(
                            PE2[:, s * 1024 + th * 512: s * 1024 + th * 512 + 512],
                            ps[bank][:, :]), EV)
                        bank_ev[bank] = vev
                P.wait("vector", EV, vev)
                q1, q2 = PE2[:, 0:1024], PE2[:, 1024:2048]
                o1 = PQ[:, (2 * h) * 1024:(2 * h) * 1024 + 1024]
                o2 = PQ[:, (2 * h + 1) * 1024:(2 * h + 1) * 1024 + 1024]
                t1, t2 = PE2[:, 2048:3072], PE2[:, 3072:4096]
                ct, st = Tb("cosq", TPC), Tb("sinq", TPC)
                P.op("vector", lambda v, t1=t1, q1=q1, ct=ct: v.tensor_mul(t1, q1, ct))
                P.op("vector", lambda v, t2=t2, q2=q2, st=st: v.tensor_mul(t2, q2, st))
                P.op("vector", lambda v, o1=o1, t1=t1, t2=t2: v.tensor_sub(o1, t1, t2))
                P.op("vector", lambda v, t1=t1, q2=q2, ct=ct: v.tensor_mul(t1, q2, ct))
                P.op("vector", lambda v, t2=t2, q1=q1, st=st: v.tensor_mul(t2, q1, st))
                vq = P.op("vector", lambda v, o2=o2, t1=t1, t2=t2: v.tensor_add(o2, t1, t2),
                          DV, wide=True)
        return vq

    mark("q_proj")
    v_q = q_proj()
    pw_free_mm = P.cur(MM)
    if DEBUG and ("d_q" in dbg or "d_kown" in dbg or "d_vown" in dbg):
        for e in Plan.ENGS:
            P.wait(e, DV, v_q)
        P.wait("gpsimd", SY, P.cur(SY))
        dbg_dump("d_q", PQ[:, :])
        dbg_dump_dram("d_kown", sp_k_own)
        dbg_dump_dram("d_vown", sp_v_own)

    def u_phase():
        v_state_mm = 0
        vev = 0
        for h in range(H):
            P.wait("sync", MM, v_state_mm)
            if h == 0:
                P.wait("sync", SY, P.cur(SY))
                P.wait("sync", DV, P.cur(DV))
                P.wait("sync", MM, P.cur(MM))
            vk = P.dma("sync", PA[:, 0:6144].rearrange("p (t c) -> p t c", c=DKH),
                       sp_k_pre.ap().rearrange("(t p) (h c) -> p t h c", p=128, c=DKH)[:, :, h],
                       SY)
            vv = P.dma("sync", PA[:, 6144:18432].rearrange("p (t c) -> p t c", c=DVH),
                       sp_v_pre.ap().rearrange("(t p) (h c) -> p t h c", p=128, c=DVH)[:, :, h],
                       SY)
            P.wait("vector", SY, vk)
            P.wait("vector", LD, v_tab)
            vdv = 0
            for t in range(NPT):
                vdv = P.op("vector", lambda v, t=t, h=h: v.tensor_scalar_mul(
                    PA[:, 18432 + t * DKH:18432 + (t + 1) * DKH],
                    PA[:, t * DKH:(t + 1) * DKH],
                    ugam_sb[:, h * NPT + t:h * NPT + t + 1]), DV)
            P.wait("tensor", DV, vdv)
            P.wait("tensor", SY, vv)
            for s in range(2):
                bank = 6 + s
                bank_wait(bank)
                for t in range(NPT):
                    P.op("tensor", lambda tt, t=t, s=s, bank=bank: tt.matmul(
                        ps[bank][:, :],
                        PA[:, 18432 + t * DKH + s * 128:18432 + t * DKH + s * 128 + 128],
                        PA[:, 6144 + t * DVH:6144 + (t + 1) * DVH],
                        start=(t == 0), stop=(t == NPT - 1)))
            v_state_mm = P.op("tensor", lambda t: t.drain(), MM)
            P.wait("scalar", MM, v_state_mm)
            for s in range(2):
                vev = P.op("scalar", lambda sc, s=s, h=h: sc.copy(
                    PBf[:, h * 1024 + s * 512: h * 1024 + (s + 1) * 512],
                    ps[6 + s][:, :]), EV)
                bank_ev[6 + s] = vev
        return vev

    mark("u_phase")
    v_u_done = u_phase()
    if DEBUG and ("d_u" in dbg or "d_gown" in dbg):
        for e in Plan.ENGS:
            P.wait(e, EV, v_u_done)
        dbg_dump("d_u", PBf[:, :])
        dbg_dump_dram("d_gown", sp_g_own)

    def retention():
        gw_bcl = PAf[:, 14336:14848]
        P.wait("sync", MM, P.cur(MM))
        vev_gw = P.dma("sync", gw_bcl,
                       bass.AP(gw_row.tensor, 0, [[0, 128], [1, DVH]]), SY)

        v_par_dv = [0, 0]
        v_og_spill = 0
        for h in range(H):
            mark(f"ret_h{h}")
            par = h % 2
            base = 8192 + par * 8192
            if h == 0:
                P.wait("sync", MM, P.cur(MM))
                P.wait("sync", DV, P.cur(DV))
            P.wait("sync", DV, v_par_dv[par])
            vkf = 0
            for s in range(2):
                vkf = P.dma("sync", PA[:, base + s * 1024:base + (s + 1) * 1024],
                            sp_k_own.ap()[:, h * DKH + s * 128:h * DKH + (s + 1) * 128],
                            SY, transpose=True)
            vkt = P.dma("sync", PA[:, base + 2048:base + 4096].rearrange(
                "p (t c) -> p t c", c=DKH),
                sp_k_own.ap().rearrange("(t p) (h c) -> p t h c", p=128, c=DKH)[:, :, h], SY)
            vvt = P.dma("sync", PA[:, base + 4096:base + 8192].rearrange(
                "p (t c) -> p t c", c=DVH),
                sp_v_own.ap().rearrange("(t p) (h c) -> p t h c", p=128, c=DVH)[:, :, h], SY)
            P.wait("scalar", EV, v_u_done)
            P.wait("scalar", SY, v_og_spill)
            v_sbf = [0, 0]
            v_sbf[0] = P.op("scalar", lambda s, h=h: s.copy(
                PE2[:, 0:1024], PBf[:, h * 1024:(h + 1) * 1024]), EV)
            v_supd, v_mask, v_qg, v_oev = 0, 0, 0, 0
            vmm_st = 0
            for c in range(NT):
                cp = c % 2
                b_sc, b_o = cp, 2 + cp
                P.wait("tensor", SY, vkf)
                P.wait("tensor", DV, v_q)
                bank_wait(b_sc)
                for s in range(2):
                    P.op("tensor", lambda t, s=s, c=c, h=h, b_sc=b_sc, base=base: t.matmul(
                        ps[b_sc][:, 0:128],
                        PA[:, base + s * 1024 + c * 128: base + s * 1024 + (c + 1) * 128],
                        PQ[:, (2 * h + s) * 1024 + c * 128:(2 * h + s) * 1024 + (c + 1) * 128],
                        start=(s == 0), stop=(s == 1)))
                vmm_sc = P.op("tensor", lambda t: t.drain(), MM)
                P.wait("vector", SY, vkt)
                vkg = P.op("vector", lambda v, c=c, h=h, base=base: v.tensor_scalar_mul(
                    PE2[:, 2048:2304],
                    PA[:, base + 2048 + c * DKH: base + 2048 + (c + 1) * DKH],
                    kgam_sb[:, h:h + 1]), DV, wide="s")
                for s in range(2):
                    v_qg = P.op("vector", lambda v, s=s, c=c, h=h: v.tensor_mul(
                        PE2[:, 2304 + s * 128:2304 + (s + 1) * 128],
                        PQ[:, (2 * h + s) * 1024 + c * 128:(2 * h + s) * 1024 + (c + 1) * 128],
                        Tb("gam", 128, h * 128)), DV, wide="s" if s == 1 else False)
                P.wait("vector", MM, vmm_sc)
                v_mask = P.op("vector", lambda v, h=h, b_sc=b_sc: v.tensor_mul(
                    PE2[:, 3712:3840], ps[b_sc][:, 0:128], Tb("dt", 128, h * 128)), DV,
                    wide="s")
                bank_dv[b_sc] = v_mask
                P.wait("tensor", DV, v_mask)
                P.wait("tensor", SY, vvt)
                bank_wait(b_o)
                P.op("tensor", lambda t, c=c, b_o=b_o, base=base: t.matmul(
                    ps[b_o][:, :], PE2[:, 3712:3840],
                    PA[:, base + 4096 + c * DVH: base + 4096 + (c + 1) * DVH],
                    start=True, stop=False))
                P.wait("tensor", DV, v_qg)
                P.wait("tensor", EV, v_sbf[cp])
                for s in range(2):
                    P.op("tensor", lambda t, s=s, cp=cp, b_o=b_o: t.matmul(
                        ps[b_o][:, :], PE2[:, 2304 + s * 128:2304 + (s + 1) * 128],
                        PE2[:, cp * 1024 + s * 512:cp * 1024 + (s + 1) * 512],
                        start=False, stop=(s == 1)))
                vmm_o = P.op("tensor", lambda t: t.drain(), MM)
                P.wait("tensor", DV, vkg)
                P.wait("tensor", DV, v_supd)
                bank_wait(4)
                bank_wait(5)
                for s in range(2):
                    P.op("tensor", lambda t, s=s, c=c, base=base: t.matmul(
                        ps[4 + s][:, :],
                        PE2[:, 2048 + s * 128:2048 + s * 128 + 128],
                        PA[:, base + 4096 + c * DVH: base + 4096 + (c + 1) * DVH],
                        start=True, stop=True))
                vmm_st = P.op("tensor", lambda t: t.drain(), MM)
                P.wait("scalar", MM, vmm_o)
                v_oev = P.op("scalar", lambda s, c=c, b_o=b_o: s.copy(
                    PAf[:, c * 512:(c + 1) * 512], ps[b_o][:, :]), EV)
                bank_ev[b_o] = v_oev
                P.wait("vector", MM, vmm_st)
                g128 = GAMMAS[h] ** C
                for s in range(2):
                    v_supd = P.op("vector", lambda v, s=s, h=h, g128=g128:
                                  v.scalar_tensor_tensor(
                                      PBf[:, h * 1024 + s * 512: h * 1024 + (s + 1) * 512],
                                      PBf[:, h * 1024 + s * 512: h * 1024 + (s + 1) * 512],
                                      g128, ps[4 + s][:, :], ALU.mult, ALU.add), DV)
                    bank_dv[4 + s] = v_supd
                if c < NT - 1:
                    ncp = (c + 1) % 2
                    P.wait("scalar", DV, v_supd)
                    v_sbf[ncp] = P.op("scalar", lambda s, h=h, ncp=ncp: s.copy(
                        PE2[:, ncp * 1024:(ncp + 1) * 1024],
                        PBf[:, h * 1024:(h + 1) * 1024]), EV)
            sqdump = PE2f[:, 1344:1856]
            vsq8 = 0
            for t in range(NT):
                vsq8 = P.op("scalar", lambda s, t=t: s.activation(
                    sqdump, PAf[:, t * 512:(t + 1) * 512], AF.Square,
                    accum_out=rms_acc[:, t:t + 1]), EV)
            P.wait("scalar", EV, vsq8)
            vln = P.op("scalar", lambda s: s.activation(rms_rstd, rms_acc, AF.Ln,
                                                        bias=EPS, scale=1.0 / DVH), EV)
            P.wait("scalar", EV, vln)
            vr = P.op("scalar", lambda s: s.activation(rms_rstd, rms_rstd, AF.Exp,
                                                       scale=-0.5), EV)
            P.wait("vector", EV, vr)
            v_on = 0
            for t in range(NT):
                v_on = P.op("vector", lambda v, t=t: v.tensor_scalar_mul(
                    PE2[:, t * 512:(t + 1) * 512], PAf[:, t * 512:(t + 1) * 512],
                    rms_rstd[:, t:t + 1]), DV)
            P.wait("sync", MM, vmm_st)
            vg = P.dma("sync", PA[:, base + 4096:base + 8192].rearrange(
                "p (t c) -> p t c", c=DVH),
                sp_g_own.ap().rearrange("(t p) (h c) -> p t h c", p=128, c=DVH)[:, :, h], SY)
            P.wait("scalar", SY, vg)
            v_og = 0
            v_og_tile = [0] * NT
            for t in range(NT):
                if t >= 4:
                    P.wait("scalar", DV, v_og_tile[t - 4])
                vs1 = P.op("scalar", lambda s, t=t, base=base: s.activation(
                    PA[:, base + 2048 + (t % 4) * 512: base + 2048 + (t % 4) * 512 + 512],
                    PA[:, base + 4096 + t * 512: base + 4096 + (t + 1) * 512],
                    AF.Silu), EV)
                P.wait("vector", EV, vs1)
                P.wait("vector", DV, v_on)
                P.wait("vector", SY, vev_gw)
                P.op("vector", lambda v, t=t, base=base: v.tensor_mul(
                    PA[:, base + (t % 4) * 512: base + (t % 4) * 512 + 512],
                    PA[:, base + 2048 + (t % 4) * 512: base + 2048 + (t % 4) * 512 + 512],
                    gw_bcl))
                v_og = P.op("vector", lambda v, t=t, base=base: v.tensor_mul(
                    PE2[:, t * 512:(t + 1) * 512],
                    PE2[:, t * 512:(t + 1) * 512],
                    PA[:, base + (t % 4) * 512: base + (t % 4) * 512 + 512]), DV)
                v_og_tile[t] = v_og
            P.wait("sync", DV, v_og)
            v_og_spill = P.dma("sync", sp_og.ap().rearrange(
                "(t p) (h c) -> p t h c", p=128, c=DVH)[:, :, h],
                PE2[:, 0:4096].rearrange("p (t c) -> p t c", c=512), SY)
            v_par_dv[par] = v_og
        return P.cur(SY)

    mark("retention")
    v_og_done = retention()
    if DEBUG and ("d_og" in dbg or "d_oraw" in dbg):
        for e in Plan.ENGS:
            P.wait(e, SY, v_og_done)
        dbg_dump_dram("d_og", sp_og)
        dbg_dump("d_oraw", PAf[:, 0:4096])

    def wo_phase():
        P.wait("sync", MM, P.cur(MM))
        P.wait("sync", EV, P.cur(EV))
        P.wait("sync", DV, P.cur(DV))
        P.wait("sync", SY, v_og_done)
        vrb = P.dma("sync", PE2[0:1, 2032:2040], sp_og.ap()[1023:1024, 4088:4096], SY)
        P.wait("sync", SY, vrb)
        vo = [0] * 32
        for hv in range(32):
            dst = PQ[:, hv * 1024:(hv + 1) * 1024] if hv < 16 else \
                PB[:, (hv - 16) * 1024:(hv - 15) * 1024]
            vo[hv] = P.dma("sync", dst, sp_og.ap()[:, hv * 128:(hv + 1) * 128],
                           SY, transpose=True)
        NSL = 16
        wfree = [pw_free_mm] * NSL

        def wo_load(it):
            ocg, hv = it // 32, it % 32
            bi = it % NSL
            P.wait("gpsimd", MM, wfree[bi])
            return P.dma("gpsimd", PW[:, bi * 512:bi * 512 + 512],
                         wo[hv * 128:(hv + 1) * 128, ocg * 512:(ocg + 1) * 512], LD)

        vws = [wo_load(0)]
        for ocg in range(4):
            vmm = 0
            for hv in range(32):
                it = ocg * 32 + hv
                bi = it % NSL
                vws.append(wo_load(it + 1) if it + 1 < 128 else gfence())
                P.wait("tensor", LD, vws[it + 1])
                P.wait("tensor", SY, vo[hv])
                if hv == 0:
                    for bank in range(8):
                        P.wait("tensor", EV, bank_ev[bank])
                        P.wait("tensor", DV, bank_dv[bank])
                rhs = PQ if hv < 16 else PB
                ro = (hv % 16) * 1024
                for oc4 in range(4):
                    for th in range(2):
                        bank = oc4 * 2 + th
                        P.op("tensor", lambda t, oc4=oc4, th=th, hv=hv, bi=bi,
                             rhs=rhs, ro=ro, bank=bank: t.matmul(
                                 ps[bank][:, :],
                                 PW[:, bi * 512 + oc4 * 128:bi * 512 + (oc4 + 1) * 128],
                                 rhs[:, ro + th * 512:ro + (th + 1) * 512],
                                 start=(hv == 0), stop=(hv == 31)))
                if hv % 8 == 7:
                    vmm = P.op("tensor", lambda t: t.drain(), MM)
                    for b2 in range(hv - 7, hv + 1):
                        wfree[(ocg * 32 + b2) % NSL] = vmm
            for oc4 in range(4):
                oc = ocg * 4 + oc4
                vx = P.dma("sync", PE2f[:, 0:1024], x_own[oc * 128:(oc + 1) * 128, :], SY)
                P.wait("vector", MM, vmm)
                P.wait("vector", SY, vx)
                vdv = 0
                for th in range(2):
                    bank = oc4 * 2 + th
                    vdv = P.op("vector", lambda v, oc=oc, th=th, bank=bank: v.tensor_add(
                        PAf[:, oc * 1024 + th * 512: oc * 1024 + (th + 1) * 512],
                        PE2f[:, th * 512:(th + 1) * 512], ps[bank][:, :]), DV)
                    bank_dv[bank] = vdv
                P.wait("sync", DV, vdv)
        wbuf_free[0] = P.cur(MM)
        wbuf_free[1] = P.cur(MM)
        return P.cur(DV)

    mark("wo")
    v_x1 = wo_phase()
    if DEBUG and "d_x1" in dbg:
        for e in Plan.ENGS:
            P.wait(e, DV, v_x1)
        dbg_dump("d_x1", PAf[:, :])

    def ln2():
        vmm_f2 = [0] * FCH
        vsq2 = [0] * FCH

        def emit_cast_sq(f):
            base = 8192 + (f % 4) * 2048
            P.wait("vector", DV, v_x1)
            if f >= 4:
                P.wait("vector", MM, vmm_f2[f - 4])
            P.op("vector", lambda v, f=f, base=base: v.tensor_copy(
                PQ[:, base:base + 1024], PAf[:, f * 1024:(f + 1) * 1024]))
            vsq2[f] = P.op("vector", lambda v, base=base: v.tensor_mul(
                PQ[:, base + 1024:base + 2048], PQ[:, base:base + 1024],
                PQ[:, base:base + 1024]), DV)

        def emit_mm2(f, gate):
            base = 8192 + (f % 4) * 2048
            P.wait("tensor", DV, gate)
            if f == 0:
                for bank in (0, 1, 2, 3, 4):
                    bank_wait(bank)
            for g in range(2):
                P.op("tensor", lambda t, f=f, g=g, base=base: t.matmul(
                    ps[g][:, :], Tb("ones", 128),
                    PQ[:, base + g * 512:base + (g + 1) * 512],
                    start=(f == 0), stop=(f == FCH - 1)))
                P.op("tensor", lambda t, f=f, g=g, base=base: t.matmul(
                    ps[2 + g][:, :], Tb("ones", 128),
                    PQ[:, base + 1024 + g * 512:base + 1024 + (g + 1) * 512],
                    start=(f == 0), stop=(f == FCH - 1)))
            vmm_f2[f] = P.op("tensor", lambda t: t.matmul(
                ps[4][:, 0:1], Tb("ones", 128), PT[:, 0:1],
                start=True, stop=True), MM)

        for f in range(FCH):
            emit_cast_sq(f)
            if f >= 2:
                emit_mm2(f - 2, vsq2[f])
        for _ in range(2):
            vpad2 = P.op("vector", lambda v: v.tensor_copy(
                PVf[:, 1160:1672], PSf[:, 512:1024]), DV)
        emit_mm2(FCH - 2, vpad2)
        emit_mm2(FCH - 1, vpad2)
        vmm = P.op("tensor", lambda t: t.drain(), MM)
        for g in range(2):
            combine_and_bcast((g, 2 + g), PQf[:, 1280 + g * 512:1280 + (g + 1) * 512],
                              PQf[:, 2304 + g * 512:2304 + (g + 1) * 512], vmm, True)
        vxn = 0
        for f in range(FCH):
            P.op("vector", lambda v, f=f: v.tensor_sub(
                PQf[:, 0:1024], PAf[:, f * 1024:(f + 1) * 1024], PQf[:, 1280:2304]))
            vxn = P.op("vector", lambda v, f=f: v.tensor_mul(
                PB[:, f * 1024:(f + 1) * 1024], PQf[:, 0:1024], PQf[:, 2304:3328]), DV,
                wide=True)
        return vxn

    mark("ln2")
    v_xn2 = ln2()
    if DEBUG and ("d_xn2" in dbg or "d_lnbc" in dbg):
        for e in Plan.ENGS:
            P.wait(e, DV, v_xn2)
        dbg_dump("d_xn2", PB[:, :])
        dbg_dump("d_lnbc", PQ[:, 2560:6656])

    def ffn():
        vbrb = P.dma("sync", br_bc, bass.AP(br_row.tensor, 0, [[0, 128], [1, H]]), SY)

        lg_all = PE2f[:, 1024:1088]
        rmx_all = PE2f[:, 1088:1096]
        ssum_all = PE2f[:, 1096:1104]
        w64_all = PE2f[:, 1104:1112]
        zz_all = PE2f[:, 1112:1176]
        eq_all = PE2f[:, 1176:1240]
        m_all = PE2f[:, 1240:1304]
        s32 = PE2f[:, 1304:1336]
        s16 = PE2f[:, 1336:1352]

        def r3(ap2, e):
            return ap2.rearrange("p (t e) -> p t e", e=e)

        def bc_pt(ap8, e):
            return bass.AP(ap8.tensor, ap8.offset,
                           [list(ap8.ap[0]), list(ap8.ap[1]), [0, e]])

        def vpad_r():
            P.op("vector", lambda v: v.tensor_copy(PE2f[:, 1536:2048],
                                                   PSf[:, 512:1024]))

        v_lg = 0
        bank_wait(6)
        for t in range(NT):
            P.wait("tensor", DV, v_xn2)
            P.wait("tensor", LD, v_tab)
            if t > 0:
                P.wait("tensor", DV, v_lg)
            for f in range(FCH):
                P.op("tensor", lambda tt, f=f, t=t: tt.matmul(
                    ps[6][:, 0:8],
                    PB[:, f * 1024 + t * 128: f * 1024 + (t + 1) * 128],
                    wrt_sb[:, f * H:(f + 1) * H],
                    start=(f == 0), stop=(f == FCH - 1)))
            vmm = P.op("tensor", lambda tt: tt.drain(), MM)
            P.wait("vector", MM, vmm)
            P.wait("vector", SY, vbrb)
            v_lg = P.op("vector", lambda v, t=t: v.tensor_add(
                PE2f[:, 1024 + t * 8:1024 + (t + 1) * 8], br_bc, ps[6][:, 0:8]), DV)
        vpad_r()
        lg3 = r3(lg_all, 8)
        P.op("vector", lambda v: v.tensor_tensor(
            r3(s32, 4), lg3[:, :, 0:4], lg3[:, :, 4:8], ALU.max))
        vpad_r()
        P.op("vector", lambda v: v.tensor_tensor(
            r3(s16, 2), r3(s32, 4)[:, :, 0:2], r3(s32, 4)[:, :, 2:4], ALU.max))
        vpad_r()
        P.op("vector", lambda v: v.tensor_tensor(
            r3(rmx_all, 1), r3(s16, 2)[:, :, 0:1], r3(s16, 2)[:, :, 1:2], ALU.max))
        vpad_r()
        P.op("vector", lambda v: v.tensor_tensor(
            r3(eq_all, 8), lg3, bc_pt(rmx_all, 8), ALU.is_equal))
        vz = P.op("vector", lambda v: v.tensor_sub(
            r3(zz_all, 8), lg3, bc_pt(rmx_all, 8)), DV, wide=True)
        P.wait("scalar", DV, vz)
        vex = P.op("scalar", lambda s: s.activation(zz_all, zz_all, AF.Exp), EV)
        P.wait("vector", EV, vex)
        zz3 = r3(zz_all, 8)
        P.op("vector", lambda v: v.tensor_add(r3(s32, 4), zz3[:, :, 0:4], zz3[:, :, 4:8]))
        vpad_r()
        P.op("vector", lambda v: v.tensor_add(
            r3(s16, 2), r3(s32, 4)[:, :, 0:2], r3(s32, 4)[:, :, 2:4]))
        vpad_r()
        P.op("vector", lambda v: v.tensor_add(
            r3(ssum_all, 1), r3(s16, 2)[:, :, 0:1], r3(s16, 2)[:, :, 1:2]))
        vpad_r()
        P.op("vector", lambda v: v.reciprocal(w64_all, ssum_all))
        vpad_r()
        P.op("vector", lambda v: v.tensor_scalar_mul(w64_all, w64_all, 1.0 / 64.0))
        vpad_r()
        v_m = P.op("vector", lambda v: v.tensor_mul(
            r3(m_all, 8), r3(eq_all, 8), bc_pt(w64_all, 8)), DV, wide=True)
        if DEBUG and "d_m" in dbg:
            for e in Plan.ENGS:
                P.wait(e, DV, v_m)
            dbg_dump("d_m", PE2f[:, 1240:1304])
        v_mfm = 0
        P.wait("tensor", LD, v_tab)
        for t in range(NT):
            P.wait("tensor", DV, v_m)
            bank_wait(7)
            P.op("tensor", lambda tt, t=t: tt.transpose(
                ps[7][0:8, 0:128], PE2f[:, 1240 + t * 8:1240 + (t + 1) * 8], ident_sb))
            vmm = P.op("tensor", lambda tt: tt.drain(), MM)
            P.wait("scalar", MM, vmm)
            P.wait("scalar", DV, v_xn2)
            v_mfm = P.op("scalar", lambda s, t=t: s.copy(
                PE2[0:8, t * 128:(t + 1) * 128], ps[7][0:8, 0:128]), EV)
            bank_ev[7] = v_mfm
        v_mrep = 0
        for s4 in range(4):
            for th in range(2):
                bank = s4 % 2
                bank_wait(bank)
                P.wait("tensor", EV, v_mfm)
                P.op("tensor", lambda t, s4=s4, th=th, bank=bank: t.matmul(
                    ps[bank][:, :], PT[0:8, PTo["expand"] + s4 * 128:PTo["expand"] + (s4 + 1) * 128],
                    PE2[0:8, th * 512:(th + 1) * 512], start=True, stop=True))
                vmm = P.op("tensor", lambda t: t.drain(), MM)
                P.wait("scalar", MM, vmm)
                v_mrep = P.op("scalar", lambda s, s4=s4, th=th, bank=bank: s.copy(
                    PQ[:, 4096 + s4 * 1024 + th * 512:4096 + s4 * 1024 + th * 512 + 512],
                    ps[bank][:, :]), EV)
                bank_ev[bank] = v_mrep
        vat = 0
        P.wait("gpsimd", MM, wbuf_free[0])
        P.wait("gpsimd", MM, wbuf_free[1])
        for f in range(FCH):
            vat = P.dma("gpsimd", PW[:, f * 512:(f + 1) * 512],
                        at_p[f * 128:(f + 1) * 128, :], LD)
        vat = gfence()
        v_hm = 0
        for s4 in range(4):
            for th in range(2):
                bank = 4 + (s4 % 2) * 2 + th
                bank_wait(bank)
                P.wait("tensor", LD, vat)
                P.wait("tensor", DV, v_xn2)
                for f in range(FCH):
                    P.op("tensor", lambda t, f=f, s4=s4, th=th, bank=bank: t.matmul(
                        ps[bank][:, :],
                        PW[:, f * 512 + s4 * 128:f * 512 + (s4 + 1) * 128],
                        PB[:, f * 1024 + th * 512: f * 1024 + (th + 1) * 512],
                        start=(f == 0), stop=(f == FCH - 1)))
                vmm = P.op("tensor", lambda t: t.drain(), MM)
                P.wait("vector", MM, vmm)
                P.wait("vector", EV, v_mrep)
                v_hm = P.op("vector", lambda v, s4=s4, th=th, bank=bank: v.tensor_mul(
                    PQ[:, 8192 + s4 * 1024 + th * 512:8192 + s4 * 1024 + th * 512 + 512],
                    PQ[:, 4096 + s4 * 1024 + th * 512:4096 + s4 * 1024 + th * 512 + 512],
                    ps[bank][:, :]), DV, wide=True)
                bank_dv[bank] = v_hm
        if DEBUG and "d_f" in dbg:
            for e in Plan.ENGS:
                P.wait(e, DV, v_hm)
            dbg_dump("d_f", PQ[:, 4096:16384])
        vbc2 = 0
        for e4 in range(4):
            vbc2 = P.dma("gpsimd", PW[:, 8192 + e4 * 2048:8192 + (e4 + 1) * 2048],
                         bcat[e4 * 128:(e4 + 1) * 128, :], LD)
        vbc2 = gfence()
        v_out = 0
        for oc in range(FCH):
            for th in range(2):
                bank = (oc % 2) * 2 + th
                bank_wait(bank)
                P.wait("tensor", DV, v_hm)
                P.wait("tensor", LD, vbc2)
                for e4 in range(4):
                    P.op("tensor", lambda t, oc=oc, th=th, e4=e4, bank=bank: t.matmul(
                        ps[bank][:, :],
                        PW[:, 8192 + e4 * 2048 + oc * 128:8192 + e4 * 2048 + (oc + 1) * 128],
                        PQ[:, 8192 + e4 * 1024 + th * 512:8192 + e4 * 1024 + th * 512 + 512],
                        start=(e4 == 0), stop=(e4 == 3)))
            vmm = P.op("tensor", lambda t: t.drain(), MM)
            P.wait("vector", MM, vmm)
            vdv = 0
            for th in range(2):
                bank = (oc % 2) * 2 + th
                vdv = P.op("vector", lambda v, oc=oc, th=th, bank=bank: v.tensor_add(
                    PAf[:, oc * 1024 + th * 512: oc * 1024 + (th + 1) * 512],
                    PAf[:, oc * 1024 + th * 512: oc * 1024 + (th + 1) * 512],
                    ps[bank][:, :]), DV)
                bank_dv[bank] = vdv
            P.wait("sync", DV, vdv)
            v_out = P.dma("sync", out_fm[oc * 128:(oc + 1) * 128, :],
                          PAf[:, oc * 1024:(oc + 1) * 1024], SY)
        return v_out

    mark("ffn")
    v_out = ffn()
    mark("end")
    P.wait("sync", SY, v_out)

    with nc.Block() as block:
        @block.sync
        def _(e):
            for fn in P.streams["sync"]:
                fn(e)

        @block.scalar
        def _(e):
            for fn in P.streams["scalar"]:
                fn(e)

        @block.vector
        def _(e):
            for fn in P.streams["vector"]:
                fn(e)

        @block.tensor
        def _(e):
            for fn in P.streams["tensor"]:
                fn(e)

        @block.gpsimd
        def _(e):
            for fn in P.streams["gpsimd"]:
                fn(e)

    es.close()
    return nc


def host_tables(q):
    start = q * TPC
    inv = 1.0 / (10000.0 ** (np.arange(0, DKH, 2, dtype=np.float64) / DKH))

    def cs(pos):
        fr = pos[:, None].astype(np.float64) * inv[None, :]
        return np.cos(fr), np.sin(fr)

    def tm(tab, ntile):
        return np.ascontiguousarray(
            tab.reshape(ntile, 128, 128).transpose(1, 0, 2).reshape(128, ntile * 128)
        ).astype(BFD)

    c_own, s_own = cs(start + np.arange(TPC))
    sc = DKH ** -0.5
    tabs = dict(
        cos_own=tm(c_own, NT), sin_own=tm(s_own, NT),
        cosq=np.ascontiguousarray((c_own * sc).T).astype(BFD),
        sinq=np.ascontiguousarray((s_own * sc).T).astype(BFD),
    )
    c_pre, s_pre = cs(np.arange(PRE))
    tabs["cos_pre"] = tm(c_pre, NPT)
    tabs["sin_pre"] = tm(s_pre, NPT)
    ii = np.arange(128, dtype=np.float64)
    dt = np.zeros((128, H * 128), np.float64)
    gt = np.zeros((128, H * 128), np.float64)
    kg = np.zeros((128, H), np.float64)
    ug = np.zeros((128, H * NPT), np.float64)
    for h in range(H):
        g = GAMMAS[h]
        diff = ii[None, :] - ii[:, None]
        dt[:, h * 128:(h + 1) * 128] = np.where(diff >= 0, g ** np.maximum(diff, 0), 0.0)
        gt[:, h * 128:(h + 1) * 128] = (g ** (ii + 1.0))[None, :]
        kg[:, h] = g ** (127.0 - ii)
        for t in range(NPT):
            l = t * 128 + ii
            ug[:, h * NPT + t] = np.where(l < start, g ** np.maximum(start - 1.0 - l, 0.0), 0.0)
    expand = np.zeros((8, 512), BFD)
    for e in range(8):
        expand[e, e * 64:(e + 1) * 64] = 1
    cvals = [EPS, 1.0 / D, -0.5, 1.0 / DVH, 1.0 / 64.0, 0.0, 1.0, 2.0] + \
        [GAMMAS[h] ** C for h in range(H)]
    tabs["constv"] = np.ascontiguousarray(
        np.broadcast_to(np.asarray(cvals, np.float32), (128, len(cvals))))
    tabs.update(
        dt_mask=dt.astype(BFD), gam_tab=gt.astype(BFD),
        kgam_v=kg.astype(np.float32), ugam_v=ug.astype(np.float32),
        ones_bf=np.ones((128, 128), BFD), ones_f=np.ones((1, 128), np.float32),
        ident_f=np.eye(128, dtype=np.float32), expand_bf=expand,
    )
    return tabs


_NC_CACHE = {}


def kernel(L0_attn_hidden_states, L0_attn_Wq, L0_attn_Wk, L0_attn_Wv, L0_attn_Wg,
           L0_attn_Wo, L0_attn_g_norm_weight, L0_ffn_A, L0_ffn_B, L0_ffn_Wr,
           L0_ffn_br):
    x = np.ascontiguousarray(np.asarray(L0_attn_hidden_states, np.float32))
    wq = np.ascontiguousarray(np.asarray(L0_attn_Wq, np.float32))
    wk = np.ascontiguousarray(np.asarray(L0_attn_Wk, np.float32))
    wv = np.ascontiguousarray(np.asarray(L0_attn_Wv, np.float32))
    wg = np.ascontiguousarray(np.asarray(L0_attn_Wg, np.float32))
    wo = np.ascontiguousarray(np.asarray(L0_attn_Wo, np.float32))
    gw = np.ascontiguousarray(np.asarray(L0_attn_g_norm_weight, np.float32).reshape(1, DVH))
    A = np.asarray(L0_ffn_A, np.float32)
    Bw = np.asarray(L0_ffn_B, np.float32)
    Wr = np.asarray(L0_ffn_Wr, np.float32)
    br = np.ascontiguousarray(np.asarray(L0_ffn_br, np.float32).reshape(1, H))

    at_p = np.ascontiguousarray(np.transpose(A, (2, 0, 1)).reshape(D, 512))
    bcat = np.ascontiguousarray(np.transpose(Bw, (0, 2, 1)).reshape(512, D))
    wrt = np.ascontiguousarray(Wr.T)
    xt = [np.ascontiguousarray(x[b].T) for b in range(Bsz)]

    if "nc" not in _NC_CACHE:
        _NC_CACHE["nc"] = build_nc()
    nc = _NC_CACHE["nc"]

    tabs = [host_tables(q) for q in range(4)]
    in_maps = []
    for c in range(8):
        b, q = c // 4, c % 4
        im = dict(
            x_own=np.ascontiguousarray(xt[b][:, q * TPC:(q + 1) * TPC]),
            x_pre=np.ascontiguousarray(xt[b][:, 0:PRE]),
            x_own_tm=np.ascontiguousarray(x[b, q * TPC:(q + 1) * TPC]),
            x_pre_tm=np.ascontiguousarray(x[b, 0:PRE]),
            wq=wq, wk=wk, wv=wv, wg=wg, wo=wo, wrt=wrt,
            br_row=br, gw_row=gw, at_p=at_p, bcat=bcat,
        )
        im.update(tabs[q])
        in_maps.append(im)

    res = run_bass_kernel_spmd(nc, in_maps, core_ids=list(range(8)))
    out = np.empty((Bsz, T, D), np.float32)
    for c in range(8):
        b, q = c // 4, c % 4
        out[b, q * TPC:(q + 1) * TPC] = res.results[c]["out_fm"].T
    if DEBUG:
        kernel.dbg_results = res.results
    return out


# version 13 (final)
# speedup vs baseline: 1.0045x; 1.0045x over previous
import numpy as np
import ml_dtypes
from contextlib import ExitStack

import concourse.bass as bass
import concourse.mybir as mybir
from concourse.bass_utils import run_bass_kernel_spmd

F32 = mybir.dt.float32
BF16 = mybir.dt.bfloat16
AF = mybir.ActivationFunctionType
ALU = mybir.AluOpType
AX = mybir.AxisListType
BFD = ml_dtypes.bfloat16

H, DKH, DVH, D = 8, 256, 512, 2048
EPS = 1e-5
Bsz, T = 2, 4096
TPC, NT = 1024, 8
PRE, PREH, NPH = 3072, 1536, 12
NPT = PRE // 128
FCH = 16
C = 128
GAMMAS = [float(1.0 - 2.0 ** (-5.0 - h)) for h in range(H)]
WBLK = 512
DEBUG = False
DEBUG_SET = None
PHASE_MARKS = []


class Plan:
    ENGS = ("sync", "scalar", "vector", "tensor", "gpsimd")

    def __init__(self, nc, es):
        self.nc = nc
        self.streams = {e: [] for e in self.ENGS}
        self.sems = {}
        self.last_wait = {}
        self.es = es

    def sem(self, name):
        h = self.es.enter_context(self.nc.semaphore(name))
        self.sems[name] = [h, 0]
        return name

    def cur(self, name):
        return self.sems[name][1]

    def emit(self, eng, fn):
        self.streams[eng].append(fn)

    def wait(self, eng, sem, v):
        if v is None or v <= 0:
            return
        assert v <= self.cur(sem), (eng, sem, v, self.cur(sem))
        if self.last_wait.get((eng, sem), -1) >= v:
            return
        self.last_wait[(eng, sem)] = v
        h = self.sems[sem][0]
        self.emit(eng, lambda e, h=h, v=v: e.wait_ge(h, v))

    def op(self, eng, build, sem=None, k=1, wide=False):
        v, h = None, None
        if sem is not None:
            s = self.sems[sem]
            s[1] += k
            v, h = s[1], s[0]
        if sem is not None and eng == "scalar":
            self.emit(eng, lambda e, build=build: build(e))
            self.emit(eng, lambda e, h=h, k=k: e.copy(
                self.dummy_s[0], self.dummy_s[1]).then_inc(h, k))
        elif sem is not None and eng == "vector":
            self.emit(eng, lambda e, build=build: build(e))
            if wide:
                p = self.pad_vs if wide == "s" else self.pad_v
                self.emit(eng, lambda e, p=p: e.tensor_copy(p[0], p[1]))
            self.emit(eng, lambda e, h=h, k=k: e.tensor_copy(
                self.dummy_v[0], self.dummy_v[1]).then_inc(h, k))
        else:
            def fn(e, build=build, h=h, k=k):
                ins = build(e)
                if h is not None:
                    ins.then_inc(h, k)
            self.emit(eng, fn)
        return v

    def dma(self, eng, out, in_, sem, **kw):
        return self.op(eng, lambda e, o=out, i=in_, kw=kw: e.dma_start(out=o, in_=i, **kw),
                       sem, 16)


def bc_free(ap, n):
    return bass.AP(ap.tensor, ap.offset, [list(ap.ap[0]), [0, n]])


def build_nc():
    nc = bass.Bass("TRN2", target_bir_lowering=False, debug=False)

    def din(name, shape, dt=F32):
        return nc.dram_tensor(name, shape, dt, kind="ExternalInput").ap()

    x_own = din("x_own", [D, TPC])
    x_pre = din("x_pre", [D, PRE])
    x_own_tm = din("x_own_tm", [TPC, D])
    x_pre_tm = din("x_pre_tm", [PRE, D])
    wq = din("wq", [D, D])
    wk = din("wk", [D, D])
    wv = din("wv", [D, 2 * D])
    wg = din("wg", [D, 2 * D])
    wo = din("wo", [2 * D, D])
    wrt = din("wrt", [D, H])
    br_row = din("br_row", [1, H])
    gw_row = din("gw_row", [1, DVH])
    at_p = din("at_p", [D, 512])
    bcat = din("bcat", [512, D])
    cos_own = din("cos_own", [128, NT * 128], BF16)
    sin_own = din("sin_own", [128, NT * 128], BF16)
    cosq = din("cosq", [128, TPC], BF16)
    sinq = din("sinq", [128, TPC], BF16)
    cos_pre = din("cos_pre", [128, NPT * 128], BF16)
    sin_pre = din("sin_pre", [128, NPT * 128], BF16)
    dt_mask = din("dt_mask", [128, H * 128], BF16)
    gam_tab = din("gam_tab", [128, H * 128], BF16)
    kgam_v = din("kgam_v", [128, H])
    ugam_v = din("ugam_v", [128, H * NPT])
    ones_bf = din("ones_bf", [128, 128], BF16)
    constv = din("constv", [128, 16])
    ones_f = din("ones_f", [1, 128])
    ident_f = din("ident_f", [128, 128])
    expand_bf = din("expand_bf", [8, 512], BF16)

    sp_k_pre = nc.dram_tensor("sp_k_pre", [PRE, D], BF16, kind="Internal")
    sp_v_pre = nc.dram_tensor("sp_v_pre", [PRE, 2 * D], BF16, kind="Internal")
    sp_k_own = nc.dram_tensor("sp_k_own", [TPC, D], BF16, kind="Internal")
    sp_v_own = nc.dram_tensor("sp_v_own", [TPC, 2 * D], BF16, kind="Internal")
    sp_g_own = nc.dram_tensor("sp_g_own", [TPC, 2 * D], BF16, kind="Internal")
    sp_og = nc.dram_tensor("sp_og", [TPC, 2 * D], BF16, kind="Internal")
    sp_stat = nc.dram_tensor("sp_stat", [2, PRE], F32, kind="Internal")
    bc_scr = nc.dram_tensor("bc_scr", [40, 512], F32, kind="Internal")
    out_fm = nc.dram_tensor("out_fm", [D, TPC], F32, kind="ExternalOutput").ap()
    dbg = {}
    if DEBUG:
        sel = DEBUG_SET
        for nm, shape, dt in [("d_xn", [128, 16384], BF16), ("d_q", [128, 16384], BF16),
                              ("d_u", [128, 8192], F32), ("d_kown", [TPC, D], BF16),
                              ("d_vown", [TPC, 2 * D], BF16), ("d_gown", [TPC, 2 * D], BF16),
                              ("d_kpre", [PRE, D], BF16), ("d_vpre", [PRE, 2 * D], BF16),
                              ("d_og", [TPC, 2 * D], BF16), ("d_x1", [128, 16384], F32),
                              ("d_oraw", [128, 4096], F32),
                              ("d_xnpreA", [128, 24576], BF16), ("d_bc", [128, 3072], BF16),
                              ("d_xpreraw", [128, 24576], BF16), ("d_scr", [40, 512], F32),
                              ("d_xn2", [128, 16384], BF16), ("d_lnbc", [128, 4096], BF16), ("d_obc", [128, 4096], BF16),
                              ("d_ps", [128, 5120], F32),
                              ("d_ps2", [128, 2048], F32), ("d_ps3", [128, 4096], BF16),
                              ("d_r", [128, 32], F32), ("d_f", [128, 12288], BF16),
                              ("d_m", [128, 64], F32), ("d_stat", [128, 16], F32)]:
            if sel is not None and nm not in sel:
                continue
            dbg[nm] = nc.dram_tensor(nm, shape, dt, kind="ExternalOutput").ap()

    es = ExitStack()
    S = es.enter_context

    PA = S(nc.sbuf_tensor("PA", [128, 32768], BF16))
    PB = S(nc.sbuf_tensor("PB", [128, 16384], BF16))
    PQ = S(nc.sbuf_tensor("PQ", [128, 16384], BF16))
    PW = S(nc.sbuf_tensor("PW", [128, 16384], BF16))
    PE2 = S(nc.sbuf_tensor("PE2", [128, 4096], BF16))
    PS = S(nc.sbuf_tensor("PS", [128, 4096], BF16))
    PT = S(nc.sbuf_tensor("PT", [128, 9856], BF16))
    PV = S(nc.sbuf_tensor("PV", [128, 4096], BF16))
    ps = [S(nc.psum_tensor(f"psb{b}", [128, 512], F32)) for b in range(8)]

    PAf = PA.bitcast(F32)
    PBf = PB.bitcast(F32)
    PQf = PQ.bitcast(F32)
    PE2f = PE2.bitcast(F32)
    PSf = PS.bitcast(F32)
    PVf = PV.bitcast(F32)

    P = Plan(nc, es)
    P.dummy_s = (PSf[:, 2040:2041], PSf[:, 2042:2043])
    P.dummy_v = (PSf[:, 2044:2045], PSf[:, 2046:2047])
    P.pad_v = (PSf[:, 1536:2040], PSf[:, 1024:1528])
    P.pad_vs = (PSf[:, 1536:1664], PSf[:, 1024:1152])
    LD = P.sem("LD")
    SY = P.sem("SY")
    MM = P.sem("MM")
    EV = P.sem("EV")
    DV = P.sem("DV")

    PTo, off = {}, 0
    for nm, w in [("cos_own", NT * 128), ("sin_own", NT * 128),
                  ("cos_pre", NPH * 128), ("sin_pre", NPH * 128),
                  ("dt", H * 128), ("gam", H * 128), ("ones", 128),
                  ("cosq", TPC), ("sinq", TPC), ("expand", 512)]:
        PTo[nm] = off
        off += w
    assert off <= 9856, off

    def Tb(nm, w, o=0):
        return PT[:, PTo[nm] + o:PTo[nm] + o + w]

    ident_sb = PSf[:, 0:128]
    kgam_sb = PSf[:, 128:136]
    ugam_sb = PSf[:, 144:144 + H * NPT]
    br_bc = PSf[:, 336:344]
    ones_f_sb = PVf[0:1, 0:128]
    m_slot = PVf[0:1, 128:640]
    r_slot = PVf[0:1, 640:1152]
    br_raw = PVf[0:1, 1152:1160]
    rms_acc = PVf[:, 1680:1688]
    rms_rstd = PVf[:, 1688:1696]
    gw_raw = PV[0:1, 3392:3392 + DVH]
    wrt_sb = PV[:, 3904:4032]

    CVALS = [EPS, 1.0 / D, -0.5, 1.0 / DVH, 1.0 / 64.0, 0.0, 1.0, 2.0] + \
        [GAMMAS[h] ** C for h in range(H)]
    constv_sb = PSf[:, 352:352 + len(CVALS)]

    bank_ev = {b: 0 for b in range(8)}
    bank_dv = {b: 0 for b in range(8)}

    PHASE_MARKS.clear()

    def mark(name):
        PHASE_MARKS.append((name, {s: P.cur(s) for s in ("LD", "SY", "MM", "EV", "DV")}))

    def bank_wait(bank):
        P.wait("tensor", EV, bank_ev[bank])
        P.wait("tensor", DV, bank_dv[bank])

    def full_barrier():
        for e in Plan.ENGS:
            for sm in (LD, SY, MM, EV, DV):
                P.wait(e, sm, P.cur(sm))

    def dbg_dump(name, src):
        if not DEBUG or name not in dbg:
            return
        v = P.dma("sync", dbg[name], src, SY)
        for e in Plan.ENGS:
            P.wait(e, SY, v)

    def dbg_dump_dram(name, src_dram):
        if not DEBUG or name not in dbg:
            return
        v = P.dma("gpsimd", dbg[name], src_dram.ap()[:, :], LD)
        for e in Plan.ENGS:
            P.wait(e, LD, v)

    for dst, src in [(Tb("cos_own", NT * 128), cos_own), (Tb("sin_own", NT * 128), sin_own),
                     (Tb("dt", H * 128), dt_mask), (Tb("gam", H * 128), gam_tab),
                     (Tb("ones", 128), ones_bf), (Tb("cosq", TPC), cosq),
                     (Tb("sinq", TPC), sinq)]:
        P.dma("gpsimd", dst, src, LD)
    P.dma("gpsimd", PT[0:8, PTo["expand"]:PTo["expand"] + 512], expand_bf, LD)
    P.dma("gpsimd", ident_sb, ident_f, LD)
    P.dma("gpsimd", ugam_sb, ugam_v, LD)
    P.dma("gpsimd", kgam_sb, kgam_v, LD)
    P.dma("gpsimd", constv_sb, constv, LD)
    for j, val in enumerate(CVALS):
        nc.const_aps.register(F32, val, constv_sb[:, j:j + 1])
    P.dma("gpsimd", br_raw, br_row, LD)
    P.dma("gpsimd", gw_raw, gw_row, LD)
    P.dma("gpsimd", wrt_sb.rearrange("p (f c) -> p f c", c=H),
          wrt.rearrange("(f p) c -> p f c", p=128), LD)
    v_tab = P.dma("gpsimd", ones_f_sb, ones_f, LD)
    for _e in ("sync", "scalar", "vector", "tensor"):
        P.wait(_e, LD, v_tab)

    def stats_tm_accum(xtm, tok0, ntiles):
        ring = 4
        scol = PVf[:, 1160:1160 + ntiles]
        qcol = PVf[:, 1184:1184 + ntiles]
        mcol = PVf[:, 1208:1208 + ntiles]
        vcol = PVf[:, 1232:1232 + ntiles]
        tcol = PVf[:, 1256:1256 + ntiles]
        dump = PV[:, 256:2304]
        vac = [0] * ntiles
        vld = {}
        P.wait("sync", EV, P.cur(EV))
        P.wait("sync", DV, P.cur(DV))

        def tm_load(t):
            if t < 0 or t >= ntiles:
                return
            if t >= ring:
                P.wait("sync", EV, vac[t - ring])
            vld[t] = P.dma("sync", PQf[:, (t % ring) * 2048:(t % ring) * 2048 + 2048],
                           xtm[tok0 + t * 128:tok0 + (t + 1) * 128, :], SY)

        tm_load(0)
        tm_load(1)
        for t in range(ntiles):
            tm_load(t + 2)
            gate = vld[t + 1] if t + 1 < ntiles else sfence()
            P.wait("scalar", SY, gate)
            sl = (t % ring) * 2048
            P.op("scalar", lambda s, sl=sl, t=t: s.activation(
                dump, PQf[:, sl:sl + 2048], AF.Copy,
                accum_out=scol[:, t:t + 1]))
            vac[t] = P.op("scalar", lambda s, sl=sl, t=t: s.activation(
                dump, PQf[:, sl:sl + 2048], AF.Square,
                accum_out=qcol[:, t:t + 1]), EV)
        vml = P.op("scalar", lambda s: s.activation(mcol, scol, AF.Copy,
                                                    scale=1.0 / D), EV)
        P.wait("vector", EV, vml)
        P.op("vector", lambda v: v.tensor_mul(tcol, mcol, mcol), DV, wide=True)
        P.op("vector", lambda v: v.tensor_scalar_mul(vcol, qcol, 1.0 / D), DV, wide=True)
        vvar = P.op("vector", lambda v: v.tensor_sub(vcol, vcol, tcol), DV, wide=True)
        P.wait("scalar", DV, vvar)
        P.op("scalar", lambda s: s.activation(vcol, vcol, AF.Ln, bias=EPS))
        return P.op("scalar", lambda s: s.activation(vcol, vcol, AF.Exp,
                                                     scale=-0.5), EV)

    def stats_tm_bcast(ntiles, mdst, rdst, dst_f32, vr):
        mcol = PVf[:, 1208:1208 + ntiles]
        vcol = PVf[:, 1232:1232 + ntiles]
        ntok = ntiles * 128
        m_t = PVf[0:ntiles, 1280:1408]
        r_t = PVf[0:ntiles, 1408:1536]
        P.wait("tensor", EV, vr)
        P.wait("tensor", LD, v_tab)
        bank_wait(6)
        bank_wait(7)
        P.op("tensor", lambda t: t.transpose(ps[6][0:ntiles, 0:128], mcol, ident_sb))
        P.op("tensor", lambda t: t.transpose(ps[7][0:ntiles, 0:128], vcol, ident_sb))
        vtr = P.op("tensor", lambda t: t.drain(), MM)
        P.wait("scalar", MM, vtr)
        P.op("scalar", lambda s: s.copy(m_t, ps[6][0:ntiles, 0:128]))
        vev = P.op("scalar", lambda s: s.copy(r_t, ps[7][0:ntiles, 0:128]), EV)
        bank_ev[6] = vev
        bank_ev[7] = vev
        P.wait("sync", EV, vev)
        o_m = bass.AP(sp_stat, 0, [[128, ntiles], [1, 128]])
        o_r = bass.AP(sp_stat, PRE, [[128, ntiles], [1, 128]])
        P.dma("sync", o_m, m_t, SY)
        vsp = P.dma("sync", o_r, r_t, SY)
        P.wait("sync", SY, vsp)
        i_m = bass.AP(sp_stat, 0, [[0, 128], [1, ntok]])
        i_r = bass.AP(sp_stat, PRE, [[0, 128], [1, ntok]])
        if dst_f32:
            P.dma("sync", mdst, i_m, SY)
            P.dma("sync", rdst, i_r, SY)
            return sfence()
        stage_m = PQf[:, 0:ntok]
        stage_r = PQf[:, 2048:2048 + ntok]
        P.dma("sync", stage_m, i_m, SY)
        P.dma("sync", stage_r, i_r, SY)
        vbb = sfence()
        P.wait("vector", SY, vbb)
        P.op("vector", lambda v: v.tensor_copy(mdst, stage_m))
        return P.op("vector", lambda v: v.tensor_copy(rdst, stage_r), DV, wide=True)

    last_bcast_mm = [0, 0]
    scr_ctr = [0]

    def combine_and_bcast(bank2, mdst, rdst, vmm, dst_f32, dbg_ps=False):
        bank_sum, bank_sq = bank2
        m_f = PSf[:, 512:1024]
        e2_f = PSf[:, 1024:1536]
        P.wait("vector", MM, vmm)
        P.wait("vector", EV, last_bcast_mm[1])
        P.op("vector", lambda v: v.tensor_scalar_mul(m_f, ps[bank_sum][:, :], 1.0 / D))
        ve = P.op("vector", lambda v: v.tensor_scalar_mul(e2_f, ps[bank_sq][:, :], 1.0 / D), DV)
        bank_dv[bank_sum] = ve
        bank_dv[bank_sq] = ve
        if DEBUG and dbg_ps and "d_ps" in dbg:
            sl = scr_ctr[0]
            scr_ctr[0] += 1
            P.op("vector", lambda v, sl=sl: v.tensor_copy(
                PQf[:, 3072 + sl * 1024:3072 + sl * 1024 + 512], m_f))
            P.op("vector", lambda v, sl=sl: v.tensor_copy(
                PQf[:, 3072 + sl * 1024 + 512:3072 + sl * 1024 + 1024], e2_f))
        sqv = PVf[:, 1160:1672]
        P.op("vector", lambda v: v.tensor_mul(sqv, m_f, m_f))
        vv = P.op("vector", lambda v: v.tensor_sub(e2_f, e2_f, sqv), DV)
        P.wait("scalar", DV, vv)
        P.op("scalar", lambda s: s.activation(e2_f, e2_f, AF.Ln, bias=EPS))
        vr = P.op("scalar", lambda s: s.activation(e2_f, e2_f, AF.Exp, scale=-0.5), EV)
        P.wait("vector", EV, vr)
        P.op("vector", lambda v: v.tensor_copy(mdst, m_f))
        vb = P.op("vector", lambda v: v.tensor_copy(rdst, e2_f), DV)
        last_bcast_mm[0] = vb
        last_bcast_mm[1] = vr
        return vb

    mbc_own = PAf[:, 12288:13312]
    rbc_own = PAf[:, 13312:14336]
    mbc_pre = PA[:, 28672:30208]
    rbc_pre = PA[:, 30208:31744]

    mbc_own = PAf[:, 12288:13312]
    rbc_own = PAf[:, 13312:14336]
    mbc_pre = PA[:, 28672:30208]
    rbc_pre = PA[:, 30208:31744]

    def phase0(half, vr_pre=None):
        if half == 0:
            vr_o = stats_tm_accum(x_own_tm, 0, NT)
            stats_tm_bcast(NT, mbc_own, rbc_own, True, vr_o)
            vr_p = stats_tm_accum(x_pre_tm, 0, NPH)
            stats_tm_bcast(NPH, mbc_pre, rbc_pre, False, vr_p)
        else:
            stats_tm_bcast(NPH, mbc_pre, rbc_pre, False, vr_pre)
        P.wait("gpsimd", MM, P.cur(MM))
        P.wait("gpsimd", DV, P.cur(DV))
        vload = []
        for f in range(FCH):
            vload.append(P.dma("gpsimd", PA[:, f * PREH:f * PREH + PREH],
                               x_pre[f * 128:(f + 1) * 128,
                                     half * PREH:half * PREH + PREH], LD))
        vload.append(gfence())
        if DEBUG and half == 0 and "d_xpreraw" in dbg:
            for e in Plan.ENGS:
                P.wait(e, LD, vload[-1])
            dbg_dump("d_xpreraw", PA[:, 0:24576])
        vout = 0
        P.wait("vector", SY, P.cur(SY))
        P.wait("vector", LD, vload[-1])
        if half == 0:
            for f in range(FCH):
                vx = P.dma("sync", PE2f[:, 0:1024], x_own[f * 128:(f + 1) * 128, :], SY)
                P.wait("vector", SY, vx)
                P.op("vector", lambda v: v.tensor_sub(PE2f[:, 1024:2048], PE2f[:, 0:1024], mbc_own))
                vout = P.op("vector", lambda v, f=f: v.tensor_mul(
                    PB[:, f * 1024:(f + 1) * 1024], PE2f[:, 1024:2048], rbc_own), DV,
                    wide=True)
                P.wait("sync", DV, vout)
        for f in range(FCH):
            P.op("vector", lambda v, f=f: v.tensor_sub(
                PE2[:, 0:1536], PA[:, f * PREH:f * PREH + PREH], mbc_pre))
            vout = P.op("vector", lambda v, f=f: v.tensor_mul(
                PA[:, f * PREH:f * PREH + PREH], PE2[:, 0:1536], rbc_pre), DV,
                wide=True)
        return vout

    wbuf_free = [0, 0]
    stage_spill = [0, 0]
    hw_slot_dv = [0, 0, 0, 0]
    fence_sb = PSf[0:1, 344:352]

    def gfence():
        return P.dma("gpsimd", fence_sb, x_own[0:1, 0:8], LD)

    def sfence():
        return P.dma("sync", fence_sb, x_own[0:1, 0:8], SY)

    def load_wblock(wap, col0, bi):
        P.wait("gpsimd", MM, wbuf_free[bi])
        vs = 0
        for f in range(FCH):
            vs = P.dma("gpsimd", PW[:, bi * 8192 + f * WBLK:bi * 8192 + (f + 1) * WBLK],
                       wap[f * 128:(f + 1) * 128, col0:col0 + WBLK], LD)
        return vs

    def load_wblock_hw(wap, col0, bi):
        P.wait("vector", MM, wbuf_free[bi])
        vs = [None] * (FCH + 1)

        def stage(f):
            if f >= FCH:
                return
            sl = (f % 4) * 512
            P.wait("sync", DV, hw_slot_dv[f % 4])
            vs[f] = P.dma("sync", PQf[:, sl:sl + 512],
                          wap[f * 128:(f + 1) * 128, col0:col0 + WBLK], SY)

        stage(0)
        last = 0
        for f in range(FCH):
            stage(f + 1)
            gate = vs[f + 1] if f + 1 < FCH else sfence()
            sl = (f % 4) * 512
            P.wait("vector", SY, gate)
            last = P.op("vector", lambda v, f=f, sl=sl, bi=bi: v.tensor_copy(
                PW[:, bi * 8192 + f * WBLK:bi * 8192 + (f + 1) * WBLK],
                PQf[:, sl:sl + 512]), DV, wide=(f == FCH - 1))
            hw_slot_dv[f % 4] = last
        return last

    def strided2(ap, lo):
        return bass.AP(ap.tensor, ap.offset + lo, [list(ap.ap[0]), [256, 2], [1, 128]])

    def tab_bc2(o):
        t = PT[:, o:o + 128]
        return bass.AP(t.tensor, t.offset, [list(t.ap[0]), [0, 2], [1, 128]])

    def rope_tm(raw, out, cosc, sinc, sem=None):
        x1, x2 = strided2(raw, 0), strided2(raw, 128)
        o1, o2 = strided2(out, 0), strided2(out, 128)
        ct, st = tab_bc2(cosc), tab_bc2(sinc)
        t1 = strided2(PE2[:, 2048:2560], 0)
        t2 = strided2(PE2[:, 2560:3072], 0)
        P.op("vector", lambda v: v.tensor_mul(t1, x1, ct))
        P.op("vector", lambda v: v.tensor_mul(t2, x2, st))
        P.op("vector", lambda v: v.tensor_sub(o1, t1, t2))
        P.op("vector", lambda v: v.tensor_mul(t1, x2, ct))
        P.op("vector", lambda v: v.tensor_mul(t2, x1, st))
        return P.op("vector", lambda v: v.tensor_add(o2, t1, t2), DV)

    def proj(wap, ncols, tiles, kind, v_xn_ready, hw=False):
        nblk = ncols // WBLK
        vwn = [None] * (nblk + 1)
        if hw:
            P.wait("sync", DV, P.cur(DV))
            P.wait("sync", EV, P.cur(EV))
        else:
            vwn[0] = load_wblock(wap, 0, 0)
        for blk in range(nblk):
            bi = blk % 2
            if hw:
                vw = load_wblock_hw(wap, blk * WBLK, bi)
                P.wait("tensor", DV, vw)
            else:
                vwn[blk + 1] = load_wblock(wap, (blk + 1) * WBLK, (blk + 1) % 2) \
                    if blk + 1 < nblk else gfence()
                P.wait("tensor", LD, vwn[blk + 1])
            P.wait("tensor", DV, v_xn_ready)
            for ti, tl in enumerate(tiles):
                bank = ti % 2
                bank_wait(bank)
                pool, stride, tt = tl["pool"], tl["stride"], tl["tt"]
                last = [None]
                for f in range(FCH):
                    P.op("tensor", lambda t, f=f, pool=pool, stride=stride, tt=tt, bi=bi,
                         bank=bank: t.matmul(
                             ps[bank][:, :],
                             pool[:, f * stride + tt * 128: f * stride + (tt + 1) * 128],
                             PW[:, bi * 8192 + f * WBLK:bi * 8192 + (f + 1) * WBLK],
                             start=(f == 0), stop=(f == FCH - 1)))
                vmm = P.op("tensor", lambda t: t.drain(), MM)
                if ti == len(tiles) - 1:
                    wbuf_free[bi] = vmm
                sl = ti % 2
                ro = sl * 1024
                P.wait("scalar", MM, vmm)
                P.wait("scalar", SY, stage_spill[sl])
                vev = P.op("scalar", lambda s, ro=ro, bank=bank: s.copy(
                    PE2[:, ro:ro + 512], ps[bank][:, :]), EV)
                bank_ev[bank] = vev
                if kind == "k":
                    P.wait("vector", EV, vev)
                    P.wait("vector", SY, stage_spill[sl])
                    vr = rope_tm(PE2[:, ro:ro + 512], PE2[:, ro + 512:ro + 1024],
                                 tl["cosc"], tl["sinc"])
                    src = PE2[:, ro + 512:ro + 1024]
                    P.wait("sync", DV, vr)
                else:
                    src = PE2[:, ro:ro + 512]
                    P.wait("sync", EV, vev)
                vsp = P.dma("sync", tl["spill"][tl["srow"]:tl["srow"] + 128,
                                                blk * WBLK:(blk + 1) * WBLK], src, SY)
                stage_spill[sl] = vsp

    def tiles_own(spill):
        return [dict(pool=PB, stride=1024, tt=t, spill=spill, srow=t * 128,
                     cosc=PTo["cos_own"] + t * 128, sinc=PTo["sin_own"] + t * 128)
                for t in range(NT)]

    def tiles_pre(spill, half):
        return [dict(pool=PA, stride=PREH, tt=t, spill=spill,
                     srow=half * PREH + t * 128,
                     cosc=PTo["cos_pre"] + t * 128, sinc=PTo["sin_pre"] + t * 128)
                for t in range(NPH)]

    mark("phase0a")
    v_xn = phase0(0)
    if DEBUG and ("d_xnpreA" in dbg or "d_bc" in dbg or "d_obc" in dbg or "d_scr" in dbg):
        for e in Plan.ENGS:
            P.wait(e, DV, v_xn)
        dbg_dump("d_xnpreA", PA[:, 0:24576])
        dbg_dump("d_bc", PA[:, 28672:31744])
        dbg_dump("d_obc", PA[:, 24576:28672])
        dbg_dump_dram("d_scr", bc_scr)
    if DEBUG and ("d_xn" in dbg or "d_stat" in dbg):
        for e in Plan.ENGS:
            P.wait(e, DV, v_xn)
        dbg_dump("d_xn", PB[:, :])
        dbg_dump("d_stat", PSf[:, 352:368].bitcast(F32) if False else constv_sb[:, 0:16])
    vct = P.dma("gpsimd", Tb("cos_pre", NPH * 128), cos_pre[:, 0:NPH * 128], LD)
    vct = P.dma("gpsimd", Tb("sin_pre", NPH * 128), sin_pre[:, 0:NPH * 128], LD)
    vct = gfence()
    P.wait("vector", LD, vct)
    mark("projA_k")
    proj(wk, D, tiles_own(sp_k_own) + tiles_pre(sp_k_pre, 0), "k", v_xn)
    mark("projA_v")
    proj(wv, 2 * D, tiles_own(sp_v_own) + tiles_pre(sp_v_pre, 0), "v", v_xn)

    mark("phase0b")
    v_xnB = phase0(1, stats_tm_accum(x_pre_tm, PREH, NPH))
    P.wait("gpsimd", DV, P.cur(DV))
    vct = P.dma("gpsimd", Tb("cos_pre", NPH * 128), cos_pre[:, NPH * 128:], LD)
    vct = P.dma("gpsimd", Tb("sin_pre", NPH * 128), sin_pre[:, NPH * 128:], LD)
    vct = gfence()
    P.wait("vector", LD, vct)
    mark("projB_k")
    proj(wk, D, tiles_pre(sp_k_pre, 1), "k", v_xnB)
    mark("projB_v")
    proj(wv, 2 * D, tiles_pre(sp_v_pre, 1), "v", v_xnB)

    if DEBUG and ("d_kpre" in dbg or "d_vpre" in dbg):
        P.wait("gpsimd", SY, P.cur(SY))
        dbg_dump_dram("d_kpre", sp_k_pre)
        dbg_dump_dram("d_vpre", sp_v_pre)

    mark("proj_g")
    proj(wg, 2 * D, tiles_own(sp_g_own), "g", v_xn)

    def q_proj():
        P.wait("scalar", SY, P.cur(SY))
        P.wait("vector", SY, P.cur(SY))
        vq = 0
        vwn = [None] * 5
        vwn[0] = load_wblock(wq, 0, 0)
        for blk in range(4):
            bi = blk % 2
            vwn[blk + 1] = load_wblock(wq, (blk + 1) * WBLK, (blk + 1) % 2) \
                if blk + 1 < 4 else gfence()
            P.wait("tensor", LD, vwn[blk + 1])
            for hh in range(2):
                h = blk * 2 + hh
                for s in range(2):
                    for th in range(2):
                        bank = 2 * s + th
                        bank_wait(bank)
                        for f in range(FCH):
                            P.op("tensor", lambda t, f=f, hh=hh, s=s, th=th, bi=bi, bank=bank:
                                 t.matmul(ps[bank][:, :],
                                          PW[:, bi * 8192 + f * WBLK + hh * 256 + s * 128:
                                             bi * 8192 + f * WBLK + hh * 256 + s * 128 + 128],
                                          PB[:, f * 1024 + th * 512: f * 1024 + (th + 1) * 512],
                                          start=(f == 0), stop=(f == FCH - 1)))
                vmm = P.op("tensor", lambda t: t.drain(), MM)
                if hh == 1:
                    wbuf_free[bi] = vmm
                P.wait("scalar", MM, vmm)
                P.wait("scalar", DV, vq)
                vev = 0
                for s in range(2):
                    for th in range(2):
                        bank = 2 * s + th
                        vev = P.op("scalar", lambda sc, s=s, th=th, bank=bank: sc.copy(
                            PE2[:, s * 1024 + th * 512: s * 1024 + th * 512 + 512],
                            ps[bank][:, :]), EV)
                        bank_ev[bank] = vev
                P.wait("vector", EV, vev)
                q1, q2 = PE2[:, 0:1024], PE2[:, 1024:2048]
                o1 = PQ[:, (2 * h) * 1024:(2 * h) * 1024 + 1024]
                o2 = PQ[:, (2 * h + 1) * 1024:(2 * h + 1) * 1024 + 1024]
                t1, t2 = PE2[:, 2048:3072], PE2[:, 3072:4096]
                ct, st = Tb("cosq", TPC), Tb("sinq", TPC)
                P.op("vector", lambda v, t1=t1, q1=q1, ct=ct: v.tensor_mul(t1, q1, ct))
                P.op("vector", lambda v, t2=t2, q2=q2, st=st: v.tensor_mul(t2, q2, st))
                P.op("vector", lambda v, o1=o1, t1=t1, t2=t2: v.tensor_sub(o1, t1, t2))
                P.op("vector", lambda v, t1=t1, q2=q2, ct=ct: v.tensor_mul(t1, q2, ct))
                P.op("vector", lambda v, t2=t2, q1=q1, st=st: v.tensor_mul(t2, q1, st))
                vq = P.op("vector", lambda v, o2=o2, t1=t1, t2=t2: v.tensor_add(o2, t1, t2),
                          DV, wide=True)
        return vq

    mark("q_proj")
    v_q = q_proj()
    pw_free_mm = P.cur(MM)
    if DEBUG and ("d_q" in dbg or "d_kown" in dbg or "d_vown" in dbg):
        for e in Plan.ENGS:
            P.wait(e, DV, v_q)
        P.wait("gpsimd", SY, P.cur(SY))
        dbg_dump("d_q", PQ[:, :])
        dbg_dump_dram("d_kown", sp_k_own)
        dbg_dump_dram("d_vown", sp_v_own)

    def u_phase():
        v_state_mm = 0
        vev = 0
        for h in range(H):
            P.wait("sync", MM, v_state_mm)
            if h == 0:
                P.wait("sync", SY, P.cur(SY))
                P.wait("sync", DV, P.cur(DV))
                P.wait("sync", MM, P.cur(MM))
            vk = P.dma("sync", PA[:, 0:6144].rearrange("p (t c) -> p t c", c=DKH),
                       sp_k_pre.ap().rearrange("(t p) (h c) -> p t h c", p=128, c=DKH)[:, :, h],
                       SY)
            vv = P.dma("sync", PA[:, 6144:18432].rearrange("p (t c) -> p t c", c=DVH),
                       sp_v_pre.ap().rearrange("(t p) (h c) -> p t h c", p=128, c=DVH)[:, :, h],
                       SY)
            P.wait("vector", SY, vk)
            P.wait("vector", LD, v_tab)
            vdv = 0
            for t in range(NPT):
                vdv = P.op("vector", lambda v, t=t, h=h: v.tensor_scalar_mul(
                    PA[:, 18432 + t * DKH:18432 + (t + 1) * DKH],
                    PA[:, t * DKH:(t + 1) * DKH],
                    ugam_sb[:, h * NPT + t:h * NPT + t + 1]), DV)
            P.wait("tensor", DV, vdv)
            P.wait("tensor", SY, vv)
            for s in range(2):
                bank = 6 + s
                bank_wait(bank)
                for t in range(NPT):
                    P.op("tensor", lambda tt, t=t, s=s, bank=bank: tt.matmul(
                        ps[bank][:, :],
                        PA[:, 18432 + t * DKH + s * 128:18432 + t * DKH + s * 128 + 128],
                        PA[:, 6144 + t * DVH:6144 + (t + 1) * DVH],
                        start=(t == 0), stop=(t == NPT - 1)))
            v_state_mm = P.op("tensor", lambda t: t.drain(), MM)
            P.wait("scalar", MM, v_state_mm)
            for s in range(2):
                vev = P.op("scalar", lambda sc, s=s, h=h: sc.copy(
                    PBf[:, h * 1024 + s * 512: h * 1024 + (s + 1) * 512],
                    ps[6 + s][:, :]), EV)
                bank_ev[6 + s] = vev
        return vev

    mark("u_phase")
    v_u_done = u_phase()
    if DEBUG and ("d_u" in dbg or "d_gown" in dbg):
        for e in Plan.ENGS:
            P.wait(e, EV, v_u_done)
        dbg_dump("d_u", PBf[:, :])
        dbg_dump_dram("d_gown", sp_g_own)

    def retention():
        gw_bcl = PAf[:, 14336:14848]
        P.wait("sync", MM, P.cur(MM))
        vev_gw = P.dma("sync", gw_bcl,
                       bass.AP(gw_row.tensor, 0, [[0, 128], [1, DVH]]), SY)

        v_par_dv = [0, 0]
        v_og_spill = 0
        for h in range(H):
            mark(f"ret_h{h}")
            par = h % 2
            base = 8192 + par * 8192
            if h == 0:
                P.wait("sync", MM, P.cur(MM))
                P.wait("sync", DV, P.cur(DV))
            P.wait("sync", DV, v_par_dv[par])
            vkf = 0
            for s in range(2):
                vkf = P.dma("sync", PA[:, base + s * 1024:base + (s + 1) * 1024],
                            sp_k_own.ap()[:, h * DKH + s * 128:h * DKH + (s + 1) * 128],
                            SY, transpose=True)
            vkt = P.dma("sync", PA[:, base + 2048:base + 4096].rearrange(
                "p (t c) -> p t c", c=DKH),
                sp_k_own.ap().rearrange("(t p) (h c) -> p t h c", p=128, c=DKH)[:, :, h], SY)
            vvt = P.dma("sync", PA[:, base + 4096:base + 8192].rearrange(
                "p (t c) -> p t c", c=DVH),
                sp_v_own.ap().rearrange("(t p) (h c) -> p t h c", p=128, c=DVH)[:, :, h], SY)
            P.wait("scalar", EV, v_u_done)
            P.wait("scalar", SY, v_og_spill)
            v_sbf = [0, 0]
            v_sbf[0] = P.op("scalar", lambda s, h=h: s.copy(
                PE2[:, 0:1024], PBf[:, h * 1024:(h + 1) * 1024]), EV)
            v_supd, v_mask, v_qg, v_oev = 0, 0, 0, 0
            vmm_st = 0
            for c in range(NT):
                cp = c % 2
                b_sc, b_o = cp, 2 + cp
                P.wait("tensor", SY, vkf)
                P.wait("tensor", DV, v_q)
                bank_wait(b_sc)
                for s in range(2):
                    P.op("tensor", lambda t, s=s, c=c, h=h, b_sc=b_sc, base=base: t.matmul(
                        ps[b_sc][:, 0:128],
                        PA[:, base + s * 1024 + c * 128: base + s * 1024 + (c + 1) * 128],
                        PQ[:, (2 * h + s) * 1024 + c * 128:(2 * h + s) * 1024 + (c + 1) * 128],
                        start=(s == 0), stop=(s == 1)))
                vmm_sc = P.op("tensor", lambda t: t.drain(), MM)
                P.wait("vector", SY, vkt)
                vkg = P.op("vector", lambda v, c=c, h=h, base=base: v.tensor_scalar_mul(
                    PE2[:, 2048:2304],
                    PA[:, base + 2048 + c * DKH: base + 2048 + (c + 1) * DKH],
                    kgam_sb[:, h:h + 1]), DV, wide="s")
                for s in range(2):
                    v_qg = P.op("vector", lambda v, s=s, c=c, h=h: v.tensor_mul(
                        PE2[:, 2304 + s * 128:2304 + (s + 1) * 128],
                        PQ[:, (2 * h + s) * 1024 + c * 128:(2 * h + s) * 1024 + (c + 1) * 128],
                        Tb("gam", 128, h * 128)), DV, wide="s" if s == 1 else False)
                P.wait("vector", MM, vmm_sc)
                v_mask = P.op("vector", lambda v, h=h, b_sc=b_sc: v.tensor_mul(
                    PE2[:, 3712:3840], ps[b_sc][:, 0:128], Tb("dt", 128, h * 128)), DV,
                    wide="s")
                bank_dv[b_sc] = v_mask
                P.wait("tensor", DV, v_mask)
                P.wait("tensor", SY, vvt)
                bank_wait(b_o)
                P.op("tensor", lambda t, c=c, b_o=b_o, base=base: t.matmul(
                    ps[b_o][:, :], PE2[:, 3712:3840],
                    PA[:, base + 4096 + c * DVH: base + 4096 + (c + 1) * DVH],
                    start=True, stop=False))
                P.wait("tensor", DV, v_qg)
                P.wait("tensor", EV, v_sbf[cp])
                for s in range(2):
                    P.op("tensor", lambda t, s=s, cp=cp, b_o=b_o: t.matmul(
                        ps[b_o][:, :], PE2[:, 2304 + s * 128:2304 + (s + 1) * 128],
                        PE2[:, cp * 1024 + s * 512:cp * 1024 + (s + 1) * 512],
                        start=False, stop=(s == 1)))
                vmm_o = P.op("tensor", lambda t: t.drain(), MM)
                P.wait("tensor", DV, vkg)
                P.wait("tensor", DV, v_supd)
                bank_wait(4)
                bank_wait(5)
                for s in range(2):
                    P.op("tensor", lambda t, s=s, c=c, base=base: t.matmul(
                        ps[4 + s][:, :],
                        PE2[:, 2048 + s * 128:2048 + s * 128 + 128],
                        PA[:, base + 4096 + c * DVH: base + 4096 + (c + 1) * DVH],
                        start=True, stop=True))
                vmm_st = P.op("tensor", lambda t: t.drain(), MM)
                P.wait("scalar", MM, vmm_o)
                v_oev = P.op("scalar", lambda s, c=c, b_o=b_o: s.copy(
                    PAf[:, c * 512:(c + 1) * 512], ps[b_o][:, :]), EV)
                bank_ev[b_o] = v_oev
                P.wait("vector", MM, vmm_st)
                g128 = GAMMAS[h] ** C
                for s in range(2):
                    v_supd = P.op("vector", lambda v, s=s, h=h, g128=g128:
                                  v.scalar_tensor_tensor(
                                      PBf[:, h * 1024 + s * 512: h * 1024 + (s + 1) * 512],
                                      PBf[:, h * 1024 + s * 512: h * 1024 + (s + 1) * 512],
                                      g128, ps[4 + s][:, :], ALU.mult, ALU.add), DV)
                    bank_dv[4 + s] = v_supd
                if c < NT - 1:
                    ncp = (c + 1) % 2
                    P.wait("scalar", DV, v_supd)
                    v_sbf[ncp] = P.op("scalar", lambda s, h=h, ncp=ncp: s.copy(
                        PE2[:, ncp * 1024:(ncp + 1) * 1024],
                        PBf[:, h * 1024:(h + 1) * 1024]), EV)
            sqdump = PE2f[:, 1344:1856]
            vsq8 = 0
            for t in range(NT):
                vsq8 = P.op("scalar", lambda s, t=t: s.activation(
                    sqdump, PAf[:, t * 512:(t + 1) * 512], AF.Square,
                    accum_out=rms_acc[:, t:t + 1]), EV)
            P.wait("scalar", EV, vsq8)
            vln = P.op("scalar", lambda s: s.activation(rms_rstd, rms_acc, AF.Ln,
                                                        bias=EPS, scale=1.0 / DVH), EV)
            P.wait("scalar", EV, vln)
            vr = P.op("scalar", lambda s: s.activation(rms_rstd, rms_rstd, AF.Exp,
                                                       scale=-0.5), EV)
            P.wait("vector", EV, vr)
            v_on = 0
            for t in range(NT):
                v_on = P.op("vector", lambda v, t=t: v.tensor_scalar_mul(
                    PE2[:, t * 512:(t + 1) * 512], PAf[:, t * 512:(t + 1) * 512],
                    rms_rstd[:, t:t + 1]), DV)
            P.wait("sync", MM, vmm_st)
            vg = P.dma("sync", PA[:, base + 4096:base + 8192].rearrange(
                "p (t c) -> p t c", c=DVH),
                sp_g_own.ap().rearrange("(t p) (h c) -> p t h c", p=128, c=DVH)[:, :, h], SY)
            P.wait("scalar", SY, vg)
            v_og = 0
            v_og_tile = [0] * NT
            for t in range(NT):
                if t >= 4:
                    P.wait("scalar", DV, v_og_tile[t - 4])
                vs1 = P.op("scalar", lambda s, t=t, base=base: s.activation(
                    PA[:, base + 2048 + (t % 4) * 512: base + 2048 + (t % 4) * 512 + 512],
                    PA[:, base + 4096 + t * 512: base + 4096 + (t + 1) * 512],
                    AF.Silu), EV)
                P.wait("vector", EV, vs1)
                P.wait("vector", DV, v_on)
                P.wait("vector", SY, vev_gw)
                P.op("vector", lambda v, t=t, base=base: v.tensor_mul(
                    PA[:, base + (t % 4) * 512: base + (t % 4) * 512 + 512],
                    PA[:, base + 2048 + (t % 4) * 512: base + 2048 + (t % 4) * 512 + 512],
                    gw_bcl))
                v_og = P.op("vector", lambda v, t=t, base=base: v.tensor_mul(
                    PE2[:, t * 512:(t + 1) * 512],
                    PE2[:, t * 512:(t + 1) * 512],
                    PA[:, base + (t % 4) * 512: base + (t % 4) * 512 + 512]), DV)
                v_og_tile[t] = v_og
            P.wait("sync", DV, v_og)
            v_og_spill = P.dma("sync", sp_og.ap().rearrange(
                "(t p) (h c) -> p t h c", p=128, c=DVH)[:, :, h],
                PE2[:, 0:4096].rearrange("p (t c) -> p t c", c=512), SY)
            v_par_dv[par] = v_og
        return P.cur(SY)

    mark("retention")
    v_og_done = retention()
    if DEBUG and ("d_og" in dbg or "d_oraw" in dbg):
        for e in Plan.ENGS:
            P.wait(e, SY, v_og_done)
        dbg_dump_dram("d_og", sp_og)
        dbg_dump("d_oraw", PAf[:, 0:4096])

    def wo_phase():
        P.wait("sync", MM, P.cur(MM))
        P.wait("sync", EV, P.cur(EV))
        P.wait("sync", DV, P.cur(DV))
        P.wait("sync", SY, v_og_done)
        vrb = P.dma("sync", PE2[0:1, 2032:2040], sp_og.ap()[1023:1024, 4088:4096], SY)
        P.wait("sync", SY, vrb)
        vo = [0] * 32
        for hv in range(32):
            dst = PQ[:, hv * 1024:(hv + 1) * 1024] if hv < 16 else \
                PB[:, (hv - 16) * 1024:(hv - 15) * 1024]
            vo[hv] = P.dma("sync", dst, sp_og.ap()[:, hv * 128:(hv + 1) * 128],
                           SY, transpose=True)
        vo.append(sfence())
        NSL = 16
        wfree = [pw_free_mm] * NSL

        def wo_load(it):
            ocg, hv = it // 32, it % 32
            bi = it % NSL
            P.wait("gpsimd", MM, wfree[bi])
            return P.dma("gpsimd", PW[:, bi * 512:bi * 512 + 512],
                         wo[hv * 128:(hv + 1) * 128, ocg * 512:(ocg + 1) * 512], LD)

        vws = [wo_load(0)]
        for ocg in range(4):
            vmm = 0
            for hv in range(32):
                it = ocg * 32 + hv
                bi = it % NSL
                vws.append(wo_load(it + 1) if it + 1 < 128 else gfence())
                P.wait("tensor", LD, vws[it + 1])
                P.wait("tensor", SY, vo[hv + 1])
                if hv == 0:
                    for bank in range(8):
                        P.wait("tensor", EV, bank_ev[bank])
                        P.wait("tensor", DV, bank_dv[bank])
                rhs = PQ if hv < 16 else PB
                ro = (hv % 16) * 1024
                for oc4 in range(4):
                    for th in range(2):
                        bank = oc4 * 2 + th
                        P.op("tensor", lambda t, oc4=oc4, th=th, hv=hv, bi=bi,
                             rhs=rhs, ro=ro, bank=bank: t.matmul(
                                 ps[bank][:, :],
                                 PW[:, bi * 512 + oc4 * 128:bi * 512 + (oc4 + 1) * 128],
                                 rhs[:, ro + th * 512:ro + (th + 1) * 512],
                                 start=(hv == 0), stop=(hv == 31)))
                if hv % 8 == 7:
                    vmm = P.op("tensor", lambda t: t.drain(), MM)
                    for b2 in range(hv - 7, hv + 1):
                        wfree[(ocg * 32 + b2) % NSL] = vmm
            for oc4 in range(4):
                oc = ocg * 4 + oc4
                vx = P.dma("sync", PE2f[:, 0:1024], x_own[oc * 128:(oc + 1) * 128, :], SY)
                P.wait("vector", MM, vmm)
                P.wait("vector", SY, vx)
                vdv = 0
                for th in range(2):
                    bank = oc4 * 2 + th
                    vdv = P.op("vector", lambda v, oc=oc, th=th, bank=bank: v.tensor_add(
                        PAf[:, oc * 1024 + th * 512: oc * 1024 + (th + 1) * 512],
                        PE2f[:, th * 512:(th + 1) * 512], ps[bank][:, :]), DV)
                    bank_dv[bank] = vdv
                P.wait("sync", DV, vdv)
        wbuf_free[0] = P.cur(MM)
        wbuf_free[1] = P.cur(MM)
        return P.cur(DV)

    mark("wo")
    v_x1 = wo_phase()
    if DEBUG and "d_x1" in dbg:
        for e in Plan.ENGS:
            P.wait(e, DV, v_x1)
        dbg_dump("d_x1", PAf[:, :])

    def ln2():
        vmm_f2 = [0] * FCH
        vsq2 = [0] * FCH

        def emit_cast_sq(f):
            base = 8192 + (f % 4) * 2048
            P.wait("vector", DV, v_x1)
            if f >= 4:
                P.wait("vector", MM, vmm_f2[f - 4])
            P.op("vector", lambda v, f=f, base=base: v.tensor_copy(
                PQ[:, base:base + 1024], PAf[:, f * 1024:(f + 1) * 1024]))
            vsq2[f] = P.op("vector", lambda v, base=base: v.tensor_mul(
                PQ[:, base + 1024:base + 2048], PQ[:, base:base + 1024],
                PQ[:, base:base + 1024]), DV)

        def emit_mm2(f, gate):
            base = 8192 + (f % 4) * 2048
            P.wait("tensor", DV, gate)
            if f == 0:
                for bank in (0, 1, 2, 3, 4):
                    bank_wait(bank)
            for g in range(2):
                P.op("tensor", lambda t, f=f, g=g, base=base: t.matmul(
                    ps[g][:, :], Tb("ones", 128),
                    PQ[:, base + g * 512:base + (g + 1) * 512],
                    start=(f == 0), stop=(f == FCH - 1)))
                P.op("tensor", lambda t, f=f, g=g, base=base: t.matmul(
                    ps[2 + g][:, :], Tb("ones", 128),
                    PQ[:, base + 1024 + g * 512:base + 1024 + (g + 1) * 512],
                    start=(f == 0), stop=(f == FCH - 1)))
            vmm_f2[f] = P.op("tensor", lambda t: t.matmul(
                ps[4][:, 0:1], Tb("ones", 128), PT[:, 0:1],
                start=True, stop=True), MM)

        for f in range(FCH):
            emit_cast_sq(f)
            if f >= 2:
                emit_mm2(f - 2, vsq2[f])
        for _ in range(2):
            vpad2 = P.op("vector", lambda v: v.tensor_copy(
                PVf[:, 1160:1672], PSf[:, 512:1024]), DV)
        emit_mm2(FCH - 2, vpad2)
        emit_mm2(FCH - 1, vpad2)
        vmm = P.op("tensor", lambda t: t.drain(), MM)
        for g in range(2):
            combine_and_bcast((g, 2 + g), PQf[:, 1280 + g * 512:1280 + (g + 1) * 512],
                              PQf[:, 2304 + g * 512:2304 + (g + 1) * 512], vmm, True)
        vxn = 0
        for f in range(FCH):
            P.op("vector", lambda v, f=f: v.tensor_sub(
                PQf[:, 0:1024], PAf[:, f * 1024:(f + 1) * 1024], PQf[:, 1280:2304]))
            vxn = P.op("vector", lambda v, f=f: v.tensor_mul(
                PB[:, f * 1024:(f + 1) * 1024], PQf[:, 0:1024], PQf[:, 2304:3328]), DV,
                wide=True)
        return vxn

    mark("ln2")
    v_xn2 = ln2()
    if DEBUG and ("d_xn2" in dbg or "d_lnbc" in dbg):
        for e in Plan.ENGS:
            P.wait(e, DV, v_xn2)
        dbg_dump("d_xn2", PB[:, :])
        dbg_dump("d_lnbc", PQ[:, 2560:6656])

    def ffn():
        vbrb = P.dma("sync", br_bc, bass.AP(br_row.tensor, 0, [[0, 128], [1, H]]), SY)

        lg_all = PE2f[:, 1024:1088]
        rmx_all = PE2f[:, 1088:1096]
        ssum_all = PE2f[:, 1096:1104]
        w64_all = PE2f[:, 1104:1112]
        zz_all = PE2f[:, 1112:1176]
        eq_all = PE2f[:, 1176:1240]
        m_all = PE2f[:, 1240:1304]
        s32 = PE2f[:, 1304:1336]
        s16 = PE2f[:, 1336:1352]

        def r3(ap2, e):
            return ap2.rearrange("p (t e) -> p t e", e=e)

        def bc_pt(ap8, e):
            return bass.AP(ap8.tensor, ap8.offset,
                           [list(ap8.ap[0]), list(ap8.ap[1]), [0, e]])

        def vpad_r():
            P.op("vector", lambda v: v.tensor_copy(PE2f[:, 1536:2048],
                                                   PSf[:, 512:1024]))

        v_lg = 0
        bank_wait(6)
        for t in range(NT):
            P.wait("tensor", DV, v_xn2)
            P.wait("tensor", LD, v_tab)
            if t > 0:
                P.wait("tensor", DV, v_lg)
            for f in range(FCH):
                P.op("tensor", lambda tt, f=f, t=t: tt.matmul(
                    ps[6][:, 0:8],
                    PB[:, f * 1024 + t * 128: f * 1024 + (t + 1) * 128],
                    wrt_sb[:, f * H:(f + 1) * H],
                    start=(f == 0), stop=(f == FCH - 1)))
            vmm = P.op("tensor", lambda tt: tt.drain(), MM)
            P.wait("vector", MM, vmm)
            P.wait("vector", SY, vbrb)
            v_lg = P.op("vector", lambda v, t=t: v.tensor_add(
                PE2f[:, 1024 + t * 8:1024 + (t + 1) * 8], br_bc, ps[6][:, 0:8]), DV)
        vpad_r()
        lg3 = r3(lg_all, 8)
        P.op("vector", lambda v: v.tensor_tensor(
            r3(s32, 4), lg3[:, :, 0:4], lg3[:, :, 4:8], ALU.max))
        vpad_r()
        P.op("vector", lambda v: v.tensor_tensor(
            r3(s16, 2), r3(s32, 4)[:, :, 0:2], r3(s32, 4)[:, :, 2:4], ALU.max))
        vpad_r()
        P.op("vector", lambda v: v.tensor_tensor(
            r3(rmx_all, 1), r3(s16, 2)[:, :, 0:1], r3(s16, 2)[:, :, 1:2], ALU.max))
        vpad_r()
        P.op("vector", lambda v: v.tensor_tensor(
            r3(eq_all, 8), lg3, bc_pt(rmx_all, 8), ALU.is_equal))
        vz = P.op("vector", lambda v: v.tensor_sub(
            r3(zz_all, 8), lg3, bc_pt(rmx_all, 8)), DV, wide=True)
        P.wait("scalar", DV, vz)
        vex = P.op("scalar", lambda s: s.activation(zz_all, zz_all, AF.Exp), EV)
        P.wait("vector", EV, vex)
        zz3 = r3(zz_all, 8)
        P.op("vector", lambda v: v.tensor_add(r3(s32, 4), zz3[:, :, 0:4], zz3[:, :, 4:8]))
        vpad_r()
        P.op("vector", lambda v: v.tensor_add(
            r3(s16, 2), r3(s32, 4)[:, :, 0:2], r3(s32, 4)[:, :, 2:4]))
        vpad_r()
        P.op("vector", lambda v: v.tensor_add(
            r3(ssum_all, 1), r3(s16, 2)[:, :, 0:1], r3(s16, 2)[:, :, 1:2]))
        vpad_r()
        P.op("vector", lambda v: v.reciprocal(w64_all, ssum_all))
        vpad_r()
        P.op("vector", lambda v: v.tensor_scalar_mul(w64_all, w64_all, 1.0 / 64.0))
        vpad_r()
        v_m = P.op("vector", lambda v: v.tensor_mul(
            r3(m_all, 8), r3(eq_all, 8), bc_pt(w64_all, 8)), DV, wide=True)
        if DEBUG and "d_m" in dbg:
            for e in Plan.ENGS:
                P.wait(e, DV, v_m)
            dbg_dump("d_m", PE2f[:, 1240:1304])
        v_mfm = 0
        P.wait("tensor", LD, v_tab)
        for t in range(NT):
            P.wait("tensor", DV, v_m)
            bank_wait(7)
            P.op("tensor", lambda tt, t=t: tt.transpose(
                ps[7][0:8, 0:128], PE2f[:, 1240 + t * 8:1240 + (t + 1) * 8], ident_sb))
            vmm = P.op("tensor", lambda tt: tt.drain(), MM)
            P.wait("scalar", MM, vmm)
            P.wait("scalar", DV, v_xn2)
            v_mfm = P.op("scalar", lambda s, t=t: s.copy(
                PE2[0:8, t * 128:(t + 1) * 128], ps[7][0:8, 0:128]), EV)
            bank_ev[7] = v_mfm
        v_mrep = 0
        for s4 in range(4):
            for th in range(2):
                bank = s4 % 2
                bank_wait(bank)
                P.wait("tensor", EV, v_mfm)
                P.op("tensor", lambda t, s4=s4, th=th, bank=bank: t.matmul(
                    ps[bank][:, :], PT[0:8, PTo["expand"] + s4 * 128:PTo["expand"] + (s4 + 1) * 128],
                    PE2[0:8, th * 512:(th + 1) * 512], start=True, stop=True))
                vmm = P.op("tensor", lambda t: t.drain(), MM)
                P.wait("scalar", MM, vmm)
                v_mrep = P.op("scalar", lambda s, s4=s4, th=th, bank=bank: s.copy(
                    PQ[:, 4096 + s4 * 1024 + th * 512:4096 + s4 * 1024 + th * 512 + 512],
                    ps[bank][:, :]), EV)
                bank_ev[bank] = v_mrep
        vat = 0
        P.wait("gpsimd", MM, wbuf_free[0])
        P.wait("gpsimd", MM, wbuf_free[1])
        for f in range(FCH):
            vat = P.dma("gpsimd", PW[:, f * 512:(f + 1) * 512],
                        at_p[f * 128:(f + 1) * 128, :], LD)
        vat = gfence()
        v_hm = 0
        for s4 in range(4):
            for th in range(2):
                bank = 4 + (s4 % 2) * 2 + th
                bank_wait(bank)
                P.wait("tensor", LD, vat)
                P.wait("tensor", DV, v_xn2)
                for f in range(FCH):
                    P.op("tensor", lambda t, f=f, s4=s4, th=th, bank=bank: t.matmul(
                        ps[bank][:, :],
                        PW[:, f * 512 + s4 * 128:f * 512 + (s4 + 1) * 128],
                        PB[:, f * 1024 + th * 512: f * 1024 + (th + 1) * 512],
                        start=(f == 0), stop=(f == FCH - 1)))
                vmm = P.op("tensor", lambda t: t.drain(), MM)
                P.wait("vector", MM, vmm)
                P.wait("vector", EV, v_mrep)
                v_hm = P.op("vector", lambda v, s4=s4, th=th, bank=bank: v.tensor_mul(
                    PQ[:, 8192 + s4 * 1024 + th * 512:8192 + s4 * 1024 + th * 512 + 512],
                    PQ[:, 4096 + s4 * 1024 + th * 512:4096 + s4 * 1024 + th * 512 + 512],
                    ps[bank][:, :]), DV, wide=True)
                bank_dv[bank] = v_hm
        if DEBUG and "d_f" in dbg:
            for e in Plan.ENGS:
                P.wait(e, DV, v_hm)
            dbg_dump("d_f", PQ[:, 4096:16384])
        vbc2 = 0
        for e4 in range(4):
            vbc2 = P.dma("gpsimd", PW[:, 8192 + e4 * 2048:8192 + (e4 + 1) * 2048],
                         bcat[e4 * 128:(e4 + 1) * 128, :], LD)
        vbc2 = gfence()
        v_out = 0
        for oc in range(FCH):
            for th in range(2):
                bank = (oc % 2) * 2 + th
                bank_wait(bank)
                P.wait("tensor", DV, v_hm)
                P.wait("tensor", LD, vbc2)
                for e4 in range(4):
                    P.op("tensor", lambda t, oc=oc, th=th, e4=e4, bank=bank: t.matmul(
                        ps[bank][:, :],
                        PW[:, 8192 + e4 * 2048 + oc * 128:8192 + e4 * 2048 + (oc + 1) * 128],
                        PQ[:, 8192 + e4 * 1024 + th * 512:8192 + e4 * 1024 + th * 512 + 512],
                        start=(e4 == 0), stop=(e4 == 3)))
            vmm = P.op("tensor", lambda t: t.drain(), MM)
            P.wait("vector", MM, vmm)
            vdv = 0
            for th in range(2):
                bank = (oc % 2) * 2 + th
                vdv = P.op("vector", lambda v, oc=oc, th=th, bank=bank: v.tensor_add(
                    PAf[:, oc * 1024 + th * 512: oc * 1024 + (th + 1) * 512],
                    PAf[:, oc * 1024 + th * 512: oc * 1024 + (th + 1) * 512],
                    ps[bank][:, :]), DV)
                bank_dv[bank] = vdv
            P.wait("sync", DV, vdv)
            v_out = P.dma("sync", out_fm[oc * 128:(oc + 1) * 128, :],
                          PAf[:, oc * 1024:(oc + 1) * 1024], SY)
        return v_out

    mark("ffn")
    v_out = ffn()
    mark("end")
    P.wait("sync", SY, v_out)

    with nc.Block() as block:
        @block.sync
        def _(e):
            for fn in P.streams["sync"]:
                fn(e)

        @block.scalar
        def _(e):
            for fn in P.streams["scalar"]:
                fn(e)

        @block.vector
        def _(e):
            for fn in P.streams["vector"]:
                fn(e)

        @block.tensor
        def _(e):
            for fn in P.streams["tensor"]:
                fn(e)

        @block.gpsimd
        def _(e):
            for fn in P.streams["gpsimd"]:
                fn(e)

    es.close()
    return nc


def host_tables(q):
    start = q * TPC
    inv = 1.0 / (10000.0 ** (np.arange(0, DKH, 2, dtype=np.float64) / DKH))

    def cs(pos):
        fr = pos[:, None].astype(np.float64) * inv[None, :]
        return np.cos(fr), np.sin(fr)

    def tm(tab, ntile):
        return np.ascontiguousarray(
            tab.reshape(ntile, 128, 128).transpose(1, 0, 2).reshape(128, ntile * 128)
        ).astype(BFD)

    c_own, s_own = cs(start + np.arange(TPC))
    sc = DKH ** -0.5
    tabs = dict(
        cos_own=tm(c_own, NT), sin_own=tm(s_own, NT),
        cosq=np.ascontiguousarray((c_own * sc).T).astype(BFD),
        sinq=np.ascontiguousarray((s_own * sc).T).astype(BFD),
    )
    c_pre, s_pre = cs(np.arange(PRE))
    tabs["cos_pre"] = tm(c_pre, NPT)
    tabs["sin_pre"] = tm(s_pre, NPT)
    ii = np.arange(128, dtype=np.float64)
    dt = np.zeros((128, H * 128), np.float64)
    gt = np.zeros((128, H * 128), np.float64)
    kg = np.zeros((128, H), np.float64)
    ug = np.zeros((128, H * NPT), np.float64)
    for h in range(H):
        g = GAMMAS[h]
        diff = ii[None, :] - ii[:, None]
        dt[:, h * 128:(h + 1) * 128] = np.where(diff >= 0, g ** np.maximum(diff, 0), 0.0)
        gt[:, h * 128:(h + 1) * 128] = (g ** (ii + 1.0))[None, :]
        kg[:, h] = g ** (127.0 - ii)
        for t in range(NPT):
            l = t * 128 + ii
            ug[:, h * NPT + t] = np.where(l < start, g ** np.maximum(start - 1.0 - l, 0.0), 0.0)
    expand = np.zeros((8, 512), BFD)
    for e in range(8):
        expand[e, e * 64:(e + 1) * 64] = 1
    cvals = [EPS, 1.0 / D, -0.5, 1.0 / DVH, 1.0 / 64.0, 0.0, 1.0, 2.0] + \
        [GAMMAS[h] ** C for h in range(H)]
    tabs["constv"] = np.ascontiguousarray(
        np.broadcast_to(np.asarray(cvals, np.float32), (128, len(cvals))))
    tabs.update(
        dt_mask=dt.astype(BFD), gam_tab=gt.astype(BFD),
        kgam_v=kg.astype(np.float32), ugam_v=ug.astype(np.float32),
        ones_bf=np.ones((128, 128), BFD), ones_f=np.ones((1, 128), np.float32),
        ident_f=np.eye(128, dtype=np.float32), expand_bf=expand,
    )
    return tabs


_NC_CACHE = {}


def kernel(L0_attn_hidden_states, L0_attn_Wq, L0_attn_Wk, L0_attn_Wv, L0_attn_Wg,
           L0_attn_Wo, L0_attn_g_norm_weight, L0_ffn_A, L0_ffn_B, L0_ffn_Wr,
           L0_ffn_br):
    x = np.ascontiguousarray(np.asarray(L0_attn_hidden_states, np.float32))
    wq = np.ascontiguousarray(np.asarray(L0_attn_Wq, np.float32))
    wk = np.ascontiguousarray(np.asarray(L0_attn_Wk, np.float32))
    wv = np.ascontiguousarray(np.asarray(L0_attn_Wv, np.float32))
    wg = np.ascontiguousarray(np.asarray(L0_attn_Wg, np.float32))
    wo = np.ascontiguousarray(np.asarray(L0_attn_Wo, np.float32))
    gw = np.ascontiguousarray(np.asarray(L0_attn_g_norm_weight, np.float32).reshape(1, DVH))
    A = np.asarray(L0_ffn_A, np.float32)
    Bw = np.asarray(L0_ffn_B, np.float32)
    Wr = np.asarray(L0_ffn_Wr, np.float32)
    br = np.ascontiguousarray(np.asarray(L0_ffn_br, np.float32).reshape(1, H))

    at_p = np.ascontiguousarray(np.transpose(A, (2, 0, 1)).reshape(D, 512))
    bcat = np.ascontiguousarray(np.transpose(Bw, (0, 2, 1)).reshape(512, D))
    wrt = np.ascontiguousarray(Wr.T)
    xt = [np.ascontiguousarray(x[b].T) for b in range(Bsz)]

    if "nc" not in _NC_CACHE:
        _NC_CACHE["nc"] = build_nc()
    nc = _NC_CACHE["nc"]

    tabs = [host_tables(q) for q in range(4)]
    in_maps = []
    for c in range(8):
        b, q = c // 4, c % 4
        im = dict(
            x_own=np.ascontiguousarray(xt[b][:, q * TPC:(q + 1) * TPC]),
            x_pre=np.ascontiguousarray(xt[b][:, 0:PRE]),
            x_own_tm=np.ascontiguousarray(x[b, q * TPC:(q + 1) * TPC]),
            x_pre_tm=np.ascontiguousarray(x[b, 0:PRE]),
            wq=wq, wk=wk, wv=wv, wg=wg, wo=wo, wrt=wrt,
            br_row=br, gw_row=gw, at_p=at_p, bcat=bcat,
        )
        im.update(tabs[q])
        in_maps.append(im)

    res = run_bass_kernel_spmd(nc, in_maps, core_ids=list(range(8)))
    out = np.empty((Bsz, T, D), np.float32)
    for c in range(8):
        b, q = c // 4, c % 4
        out[b, q * TPC:(q + 1) * TPC] = res.results[c]["out_fm"].T
    if DEBUG:
        kernel.dbg_results = res.results
    return out
